# version 08
# baseline (speedup 1.0000x reference)
import numpy as np
import concourse.bass as bass
import concourse.mybir as mybir
from concourse.bass_utils import run_bass_kernel_spmd
from contextlib import ExitStack

F32 = mybir.dt.float32
BF16 = mybir.dt.bfloat16
ALU = mybir.AluOpType
ACTF = mybir.ActivationFunctionType
AX = mybir.AxisListType

D = 4096
CTX = 2048
OWN = 1024
NCK = 32
NKV = 8
NQH = 32
EPS = 1e-5
INV_D = 1.0 / D
QB = 256
NB = 4
NST = 10


class Buf:
    __slots__ = ("w", "r")

    def __init__(self):
        self.w = None
        self.r = []


class Prog:
    ENGINES = ("sync", "gpsimd", "tensor", "vector", "scalar")
    ESEM = {"sync": "sy", "gpsimd": "gq", "tensor": "te", "vector": "ve", "scalar": "se"}

    def __init__(self, nc, es):
        self.nc = nc
        self.es = es
        self.sems = {}
        self.cnt = {}
        self.q = {e: [] for e in self.ENGINES}
        self.last_wait = {e: {} for e in self.ENGINES}
        for name in ("sy", "gq", "te", "ve", "se", "bar"):
            self.sems[name] = es.enter_context(nc.semaphore(name))
            self.cnt[name] = 0
        self.nbar = 0

    def _emit(self, eng, build, waits, inc, inc_by):
        byname = {}
        for s, v in waits:
            if v is None or v <= 0:
                continue
            if s == self.ESEM[eng] and eng == "tensor":
                continue
            if v > byname.get(s, 0):
                byname[s] = v
        wl = []
        for s, v in byname.items():
            if self.last_wait[eng].get(s, 0) >= v:
                continue
            self.last_wait[eng][s] = v
            wl.append((self.sems[s], v))
        self.cnt[inc] += inc_by
        sem_inc = self.sems[inc]

        def run(e, _b=build, _w=wl, _s=sem_inc, _ib=inc_by):
            for sem, v in _w:
                e.wait_ge(sem, v)
            r = _b(e)
            r.then_inc(_s, _ib)

        self.q[eng].append(run)
        return (inc, self.cnt[inc])

    def op(self, eng, build, reads=(), writes=(), extra=(), dma=False):
        waits = list(extra)
        for b in reads:
            if b.w is not None:
                waits.append(b.w)
        for b in writes:
            if b.w is not None:
                waits.append(b.w)
            waits.extend(b.r)
        inc_by = 16 if dma else 1
        tok = self._emit(eng, build, waits, self.ESEM[eng], inc_by)
        for b in reads:
            b.r.append(tok)
        for b in writes:
            b.w = tok
            b.r = []
        return tok

    def fill(self, eng, pairs, reads=(), writes=()):
        waits = []
        for bf in reads:
            if bf.w is not None:
                waits.append(bf.w)
        for bf in writes:
            if bf.w is not None:
                waits.append(bf.w)
            waits.extend(bf.r)
        tok = None
        for i, (o_ap, i_ap) in enumerate(pairs):
            tok = self._emit(eng, (lambda e, _o=o_ap, _i=i_ap: e.dma_start(out=_o, in_=_i)),
                             (waits if i == 0 else []), self.ESEM[eng], 16)
        for bf in reads:
            bf.r.append(tok)
        for bf in writes:
            bf.w = tok
            bf.r = []
        return tok

    def dma(self, eng, out_ap, in_ap, reads=(), writes=(), extra=()):
        return self.op(eng, lambda e, _o=out_ap, _i=in_ap: e.dma_start(out=_o, in_=_i),
                       reads=reads, writes=writes, extra=extra, dma=True)

    def barrier(self):
        self.nbar += 1
        tgt = 5 * self.nbar
        for eng in self.ENGINES:
            extra = [(self.ESEM[eng], self.cnt[self.ESEM[eng]])]
            self._emit(eng, lambda e: e.nop(), extra, "bar", 1)
        for eng in self.ENGINES:
            self._emit(eng, lambda e: e.nop(), [("bar", tgt)], self.ESEM[eng],
                       16 if eng in ("sync", "gpsimd") else 1)

    def flush(self):
        with self.nc.Block() as block:
            for eng in self.ENGINES:
                ops = self.q[eng]
                if not ops:
                    continue

                def body(e, _ops=tuple(ops)):
                    for f in _ops:
                        f(e)

                getattr(block, eng)(body)
                self.q[eng] = []


def build_program(debug=False):
    nc = bass.Bass("TRN2", target_bir_lowering=False)
    DK = {"kind": "ExternalOutput"} if debug else {}

    xin = nc.dram_tensor("xin", [CTX, D], F32, kind="ExternalInput").ap()
    wq_d = nc.dram_tensor("wq", [D, D], F32, kind="ExternalInput").ap()
    wk_d = nc.dram_tensor("wk", [D, 1024], F32, kind="ExternalInput").ap()
    wv_d = nc.dram_tensor("wv", [D, 1024], F32, kind="ExternalInput").ap()
    wo_d = nc.dram_tensor("wo", [D, D], F32, kind="ExternalInput").ap()
    wb_d = nc.dram_tensor("wb", [D, D], F32, kind="ExternalInput").ap()
    wr_d = nc.dram_tensor("wr", [128, NCK * 8], F32, kind="ExternalInput").ap()
    la_d = nc.dram_tensor("la", [128, NCK * 128], F32, kind="ExternalInput").ap()
    lb_d = nc.dram_tensor("lb", [128, D], F32, kind="ExternalInput").ap()
    bb_d = nc.dram_tensor("bb", [128, NCK], F32, kind="ExternalInput").ap()
    cosq_d = nc.dram_tensor("cosq", [128, OWN], F32, kind="ExternalInput").ap()
    sinq_d = nc.dram_tensor("sinq", [128, OWN], F32, kind="ExternalInput").ap()
    cosk_d = nc.dram_tensor("cosk", [128, CTX], F32, kind="ExternalInput").ap()
    sink_d = nc.dram_tensor("sink", [128, CTX], F32, kind="ExternalInput").ap()
    masks_d = nc.dram_tensor("masks", [128, NB * NST * QB], F32, kind="ExternalInput").ap()
    idn_d = nc.dram_tensor("idn", [128, 128], F32, kind="ExternalInput").ap()
    e8_d = nc.dram_tensor("e8", [8, 128], F32, kind="ExternalInput").ap()
    yout = nc.dram_tensor("yout", [OWN, D], F32, kind="ExternalOutput").ap()

    x0sc = nc.dram_tensor("x0sc", [NCK, 128, CTX], F32, **DK).ap()
    x1sc = nc.dram_tensor("x1sc", [NCK, 128, CTX], F32, **DK).ap()
    x2sc = nc.dram_tensor("x2sc", [NCK, 128, OWN], F32, **DK).ap()
    khs = nc.dram_tensor("khs", [NKV, 128, CTX], BF16, **DK).ap()
    kls = nc.dram_tensor("kls", [NKV, 128, CTX], BF16, **DK).ap()
    vth = nc.dram_tensor("vth", [CTX // 128, 128, 1024], BF16, **DK).ap()
    vtl = nc.dram_tensor("vtl", [CTX // 128, 128, 1024], BF16, **DK).ap()
    qhs = nc.dram_tensor("qhs", [NQH, 128, OWN], BF16, **DK).ap()
    qls = nc.dram_tensor("qls", [NQH, 128, OWN], BF16, **DK).ap()
    osh = nc.dram_tensor("osh", [NCK, 128, OWN], BF16, **DK).ap()
    osl = nc.dram_tensor("osl", [NCK, 128, OWN], BF16, **DK).ap()
    if debug:
        dbg_statc = nc.dram_tensor("dbg_statc", [128, 16], F32, kind="ExternalOutput").ap()
        dbg_rrow = nc.dram_tensor("dbg_rrow", [1, 512], F32, kind="ExternalOutput").ap()
        dbg_mrow = nc.dram_tensor("dbg_mrow", [1, 512], F32, kind="ExternalOutput").ap()
        dbg_rbc = nc.dram_tensor("dbg_rbc", [128, 512], F32, kind="ExternalOutput").ap()
        dbg_mrbc = nc.dram_tensor("dbg_mrbc", [128, 512], F32, kind="ExternalOutput").ap()
        dbg_xnh = nc.dram_tensor("dbg_xnh", [128, NCK * 512], BF16, kind="ExternalOutput").ap()
        dbg_xnl = nc.dram_tensor("dbg_xnl", [128, NCK * 512], BF16, kind="ExternalOutput").ap()
        dbg_hp = nc.dram_tensor("dbg_hp", [128, 512], BF16, kind="ExternalOutput").ap()
        dbg_wbc = nc.dram_tensor("dbg_wbc", [128, 512], F32, kind="ExternalOutput").ap()

    es = ExitStack()
    P = Prog(nc, es)

    def SB(stack, name, shape, dt):
        t = stack.enter_context(nc.sbuf_tensor(name, shape, dt))
        return t, Buf()

    def PS(stack, name, shape):
        t = stack.enter_context(nc.psum_tensor(name, shape, F32))
        return t, Buf()

    lat_h, b_lat_h = SB(es, "lat_h", [128, NCK * 128], BF16)
    lat_l, b_lat_l = SB(es, "lat_l", [128, NCK * 128], BF16)
    lbt_h, b_lbt_h = SB(es, "lbt_h", [128, D], BF16)
    lbt_l, b_lbt_l = SB(es, "lbt_l", [128, D], BF16)
    wrt_h, b_wrt_h = SB(es, "wrt_h", [128, NCK * 8], BF16)
    wrt_l, b_wrt_l = SB(es, "wrt_l", [128, NCK * 8], BF16)
    t_id, b_id = SB(es, "t_id", [128, 128], F32)
    t_e8, b_e8 = SB(es, "t_e8", [8, 128], F32)
    ones_b, b_ones_b = SB(es, "ones_b", [128, 1], BF16)
    ones_c, b_ones_c = SB(es, "ones_c", [128, 1], F32)
    ones_r, b_ones_r = SB(es, "ones_r", [1, 128], F32)
    neg8, b_neg8 = SB(es, "neg8", [128, 8], F32)
    bbc, b_bbc = SB(es, "bbc", [128, NCK], F32)
    nq2, b_nq2 = SB(es, "nq2", [1, 512], F32)
    nk2, b_nk2 = SB(es, "nk2", [1, 512], F32)
    biasc, b_biasc = SB(es, "biasc", [128, 1], F32)
    rrow, b_rrow = SB(es, "rrow", [1, 512], F32)
    mrow, b_mrow = SB(es, "mrow", [1, 512], F32)
    srow, b_srow = SB(es, "srow", [1, 512], F32)
    ssrow, b_ssrow = SB(es, "ssrow", [1, 512], F32)
    nrow, b_nrow = SB(es, "nrow", [1, 512], F32)
    r1a, b_r1a = SB(es, "r1a", [128, 1], F32)
    epsc, b_eps = SB(es, "epsc", [128, 1], F32)
    r1b, b_r1b = SB(es, "r1b", [128, 1], F32)

    b_x0sc, b_x1sc, b_x2sc = Buf(), Buf(), Buf()
    b_khs, b_kls, b_vth, b_vtl = Buf(), Buf(), Buf(), Buf()
    b_qhs, b_qls, b_osh, b_osl = Buf(), Buf(), Buf(), Buf()

    P.dma("sync", t_id[:], idn_d[:, :], writes=[b_id])
    P.dma("sync", t_e8[:], e8_d[:, :], writes=[b_e8])
    P.dma("sync", bbc[:], bb_d[:, :], writes=[b_bbc])
    P.op("vector", lambda v: v.memset(ones_b[:], 1.0), writes=[b_ones_b])
    P.op("vector", lambda v: v.memset(ones_c[:], 1.0), writes=[b_ones_c])
    P.op("vector", lambda v: v.memset(ones_r[:], 1.0), writes=[b_ones_r])
    P.op("vector", lambda v: v.memset(neg8[:], -1e30), writes=[b_neg8])
    P.op("vector", lambda v: v.memset(epsc[:], EPS), writes=[b_eps])
    P.op("vector", lambda v: v.memset(nq2[:], 0.0), writes=[b_nq2])
    P.op("vector", lambda v: v.memset(nk2[:], 0.0), writes=[b_nk2])

    pre = ExitStack()
    pw32, b_pw32 = SB(pre, "pw32", [128, NCK * 128], F32)
    for dst_h, bh_, dst_l, bl_, src, ncols in (
            (lat_h, b_lat_h, lat_l, b_lat_l, la_d[:, :], NCK * 128),
            (lbt_h, b_lbt_h, lbt_l, b_lbt_l, lb_d[:, :], D),
            (wrt_h, b_wrt_h, wrt_l, b_wrt_l, wr_d[:, :], NCK * 8)):
        P.dma("sync", pw32[:, 0:ncols], src, writes=[b_pw32])
        P.dma("gpsimd", dst_h[:, 0:ncols], pw32[:, 0:ncols], reads=[b_pw32], writes=[bh_])
        P.op("vector", lambda v, _h=dst_h, _l=dst_l, _n=ncols: v.tensor_tensor(
            out=_l[:, 0:_n], in0=pw32[:, 0:_n], in1=_h[:, 0:_n], op=ALU.subtract),
            reads=[b_pw32, bh_], writes=[bl_])
    P.flush()
    pre.close()

    def alloc_ffn_stack(tag):
        ph = ExitStack()
        d = {"_stack": ph}
        d["x0tm"] = SB(ph, f"x0tm{tag}", [128, D], F32)
        d["rbc"] = SB(ph, f"rbc{tag}", [128, 512], F32)
        d["mrbc"] = SB(ph, f"mrbc{tag}", [128, 512], F32)
        d["xn_h"] = SB(ph, f"xn_h{tag}", [128, NCK * 512], BF16)
        d["xn_l"] = SB(ph, f"xn_l{tag}", [128, NCK * 512], BF16)
        d["wf32"] = [SB(ph, f"wf32{tag}{i}", [128, 16 * 128], F32) for i in range(2)]
        d["wp_h"] = [SB(ph, f"wp_h{tag}{i}", [128, 16 * 128], BF16) for i in range(2)]
        d["wp_l"] = [SB(ph, f"wp_l{tag}{i}", [128, 16 * 128], BF16) for i in range(2)]
        d["xr"] = [SB(ph, f"xr{tag}{i}", [128, 512], F32) for i in range(2)]
        d["ta"] = SB(ph, f"ta{tag}", [128, 512], F32)
        d["tb"] = SB(ph, f"tb{tag}", [128, 512], F32)
        d["tc"] = SB(ph, f"tc{tag}", [128, 512], F32)
        d["td"] = SB(ph, f"td{tag}", [128, 512], F32)
        d["sq32"] = SB(ph, f"sq32{tag}", [128, 512], F32)
        d["ch_h"] = [SB(ph, f"ch_h{tag}{i}", [128, 512], BF16) for i in range(2)]
        d["ch_l"] = [SB(ph, f"ch_l{tag}{i}", [128, 512], BF16) for i in range(2)]
        d["hp_h"] = SB(ph, f"hp_h{tag}", [128, 512], BF16)
        d["hp_l"] = SB(ph, f"hp_l{tag}", [128, 512], BF16)
        d["ltm"] = SB(ph, f"ltm{tag}", [128, 32], F32)
        d["wtm"] = SB(ph, f"wtm{tag}", [128, 32], F32)
        d["rt8a"] = SB(ph, f"rt8a{tag}", [128, 8], F32)
        d["rt8b"] = SB(ph, f"rt8b{tag}", [128, 8], F32)
        d["rt8m"] = SB(ph, f"rt8m{tag}", [128, 8], mybir.dt.uint8)
        d["rt8c"] = SB(ph, f"rt8c{tag}", [128, 8], F32)
        d["wfm8"] = SB(ph, f"wfm8{tag}", [8, 512], F32)
        d["wbc"] = SB(ph, f"wbc{tag}", [128, 512], F32)
        d["xds"] = SB(ph, f"xds{tag}", [128, 2048], F32)
        d["ck"] = SB(ph, f"ck{tag}", [128, 512], F32)
        d["sk"] = SB(ph, f"sk{tag}", [128, 512], F32)
        d["cq"] = SB(ph, f"cq{tag}", [128, 512], F32)
        d["squ"] = SB(ph, f"squ{tag}", [128, 512], F32)
        d["statc"] = SB(ph, f"statc{tag}", [128, 16], F32)
        d["pm"] = [PS(ph, f"pm{tag}{i}", [128, 512]) for i in range(2)]
        d["paux"] = PS(ph, f"paux{tag}", [128, 512])
        d["ptp"] = [PS(ph, f"ptp{tag}{i}", [128, 512]) for i in range(2)]
        d["prow0"] = PS(ph, f"prow0{tag}", [1, 512])
        d["prow1"] = PS(ph, f"prow1{tag}", [1, 512])
        d["pnorm"] = PS(ph, f"pnorm{tag}", [1, 512])
        return d

    def ln_rows(d):
        pr0, bpr0 = d["prow0"]
        pr1, bpr1 = d["prow1"]
        P.op("vector", lambda v: v.tensor_copy(srow[:], pr0[:]), reads=[bpr0], writes=[b_srow])
        P.op("vector", lambda v: v.tensor_copy(ssrow[:], pr1[:]), reads=[bpr1], writes=[b_ssrow])
        P.op("scalar", lambda s: s.mul(out=srow[:], in_=srow[:], mul=INV_D),
             reads=[b_srow], writes=[b_srow])
        P.op("vector", lambda v: v.tensor_tensor(out=mrow[:], in0=srow[:], in1=srow[:], op=ALU.mult),
             reads=[b_srow], writes=[b_mrow])
        P.op("vector", lambda v: v.scalar_tensor_tensor(
            out=ssrow[:], in0=ssrow[:], scalar=INV_D, in1=mrow[:], op0=ALU.mult, op1=ALU.subtract),
            reads=[b_ssrow, b_mrow], writes=[b_ssrow])
        P.op("scalar", lambda s: s.activation(out=ssrow[:], in_=ssrow[:], func=ACTF.Sqrt,
                                              bias=epsc[0:1, 0:1]),
             reads=[b_ssrow, b_eps], writes=[b_ssrow])
        P.op("vector", lambda v: v.reciprocal(rrow[:], ssrow[:]), reads=[b_ssrow], writes=[b_rrow])
        P.op("vector", lambda v: v.tensor_tensor(out=mrow[:], in0=srow[:], in1=rrow[:], op=ALU.mult),
             reads=[b_srow, b_rrow], writes=[b_mrow])

    def bcast_rows(d):
        for bank, (src, bsrc), (dst, bdst) in ((0, (rrow, b_rrow), d["rbc"]),
                                               (1, (mrow, b_mrow), d["mrbc"])):
            tp, btp = d["ptp"][bank]

            def b(t, _tp=tp, _s=src):
                r = None
                for qd in range(4):
                    r = t.matmul(_tp[:, qd * 128:(qd + 1) * 128], ones_r[:],
                                 _s[:, qd * 128:(qd + 1) * 128], start=(qd == 0), stop=(qd == 3))
                return r
            P.op("tensor", b, reads=[bsrc, b_ones_r], writes=[btp])
            P.op("vector", lambda v, _tp=tp, _d=dst: v.tensor_copy(_d[:], _tp[:]),
                 reads=[btp], writes=[bdst])

    def stat_mms(d, chunk_ap, bchunk, first, last):
        sq, bsq = d["sq32"]
        pr0, bpr0 = d["prow0"]
        pr1, bpr1 = d["prow1"]
        P.op("scalar", lambda s, _i=chunk_ap: s.activation(out=sq[:], in_=_i, func=ACTF.Square),
             reads=[bchunk], writes=[bsq])
        P.op("tensor", lambda t, _i=chunk_ap, _f=first, _l=last: t.matmul(
            pr0[:], ones_c[:], _i, start=_f, stop=_l, skip_group_check=True),
            reads=[bchunk, b_ones_c], writes=[bpr0])
        P.op("tensor", lambda t, _f=first, _l=last: t.matmul(
            pr1[:], ones_c[:], sq[:], start=_f, stop=_l, skip_group_check=True),
            reads=[bsq, b_ones_c], writes=[bpr1])

    def split32(src_ap, bsrc, dh_ap, dl_ap, bdst_h, bdst_l, via_act):
        if via_act:
            P.op("scalar", lambda s, _o=dh_ap, _i=src_ap: s.copy(out=_o, in_=_i),
                 reads=[bsrc], writes=[bdst_h])
        else:
            P.dma("gpsimd", dh_ap, src_ap, reads=[bsrc], writes=[bdst_h])
        P.op("vector", lambda v, _o=dl_ap, _i=src_ap, _h=dh_ap: v.tensor_tensor(
            out=_o, in0=_i, in1=_h, op=ALU.subtract), reads=[bsrc, bdst_h], writes=[bdst_l])

    def norm_split(d, src_sc, bsrc, col0):
        rbc, brbc = d["rbc"]
        mrbc, bmrbc = d["mrbc"]
        xnh, bxnh = d["xn_h"]
        xnl, bxnl = d["xn_l"]
        ta, bta = d["ta"]
        for c in range(NCK):
            xr, bxr = d["xr"][c % 2]
            P.dma("sync", xr[:], src_sc[c, :, col0:col0 + 512], reads=[bsrc], writes=[bxr])
            P.op("vector", lambda v, _x=xr: v.tensor_tensor(
                out=ta[:], in0=_x[:], in1=rbc[:], op=ALU.mult),
                reads=[bxr, brbc], writes=[bta])
            P.op("vector", lambda v: v.tensor_tensor(
                out=ta[:], in0=ta[:], in1=mrbc[:], op=ALU.subtract),
                reads=[bmrbc, bta], writes=[bta])
            split32(ta[:], bta, xnh[:, c * 512:(c + 1) * 512],
                    xnl[:, c * 512:(c + 1) * 512], bxnh, bxnl, via_act=(c % 2 == 0))

    def router_and_h(d):
        xnh, bxnh = d["xn_h"]
        xnl, bxnl = d["xn_l"]
        pa, bpa = d["paux"]
        tb, btb = d["tb"]
        ltm, bltm = d["ltm"]
        wtm, bwtm = d["wtm"]
        rt8a, b8a = d["rt8a"]
        rt8b, b8b = d["rt8b"]
        rt8m, b8m = d["rt8m"]
        rt8c, b8c = d["rt8c"]
        wfm8, bwfm = d["wfm8"]
        wbc, bwbc = d["wbc"]
        hph, bhph = d["hp_h"]
        hpl, bhpl = d["hp_l"]
        tp0, btp0 = d["ptp"][0]
        tp1, btp1 = d["ptp"][1]

        def blog(t):
            r = None
            for c in range(NCK):
                lh = wrt_h[:, c * 8:(c + 1) * 8]
                ll = wrt_l[:, c * 8:(c + 1) * 8]
                rh = xnh[:, c * 512:(c + 1) * 512]
                rl = xnl[:, c * 512:(c + 1) * 512]
                t.matmul(pa[0:8, :], lh, rh, start=(c == 0), stop=False)
                t.matmul(pa[0:8, :], ll, rh, start=False, stop=False)
                r = t.matmul(pa[0:8, :], lh, rl, start=False, stop=(c == NCK - 1))
            return r
        P.op("tensor", blog, reads=[bxnh, bxnl, b_wrt_h, b_wrt_l], writes=[bpa])
        P.op("vector", lambda v: v.tensor_copy(tb[0:8, :], pa[0:8, :]), reads=[bpa], writes=[btb])
        for stt in range(4):
            P.op("tensor", lambda t, _s=stt: t.transpose(
                tp0[0:128, 0:8], tb[0:8, _s * 128:(_s + 1) * 128], t_id[0:8, 0:8]),
                reads=[btb, b_id], writes=[btp0])
            P.op("vector", lambda v, _s=stt: v.tensor_copy(ltm[:, _s * 8:(_s + 1) * 8], tp0[0:128, 0:8]),
                 reads=[btp0], writes=[bltm])
        for stt in range(4):
            l8 = ltm[:, stt * 8:(stt + 1) * 8]
            w8 = wtm[:, stt * 8:(stt + 1) * 8]
            P.op("vector", lambda v, _l=l8: v.reduce_max(r1a[:], _l, axis=AX.X),
                 reads=[bltm], writes=[b_r1a])
            P.op("vector", lambda v, _l=l8: v.tensor_scalar(
                out=rt8a[:], in0=_l, scalar1=r1a[:, 0:1], scalar2=None, op0=ALU.subtract),
                reads=[bltm, b_r1a], writes=[b8a])
            P.op("vector", lambda v: v.tensor_single_scalar(
                out=rt8m[:], in_=rt8a[:], scalar=0.0, op=ALU.is_lt), reads=[b8a], writes=[b8m])
            P.op("vector", lambda v: v.select(out=rt8c[:], mask=rt8m[:], on_true=rt8a[:], on_false=neg8[:]),
                 reads=[b8a, b8m, b_neg8], writes=[b8c])
            P.op("vector", lambda v: v.reduce_max(r1b[:], rt8c[:], axis=AX.X),
                 reads=[b8c], writes=[b_r1b])
            P.op("vector", lambda v: v.tensor_scalar(
                out=rt8b[:], in0=rt8a[:], scalar1=r1b[:, 0:1], scalar2=None, op0=ALU.is_ge),
                reads=[b8a, b_r1b], writes=[b8b])
            P.op("scalar", lambda s: s.activation(out=r1a[:], in_=r1b[:], func=ACTF.Exp),
                 reads=[b_r1b], writes=[b_r1a])
            P.op("scalar", lambda s: s.activation(out=rt8a[:], in_=rt8a[:], func=ACTF.Exp),
                 reads=[b8a], writes=[b8a])
            P.op("vector", lambda v: v.tensor_scalar_add(r1a[:], r1a[:], 1.0),
                 reads=[b_r1a], writes=[b_r1a])
            P.op("vector", lambda v: v.reciprocal(r1a[:], r1a[:]), reads=[b_r1a], writes=[b_r1a])
            P.op("vector", lambda v, _w=w8: v.scalar_tensor_tensor(
                out=_w, in0=rt8a[:], scalar=r1a[:, 0:1], in1=rt8b[:], op0=ALU.mult, op1=ALU.mult),
                reads=[b8a, b_r1a, b8b], writes=[bwtm])
        for stt in range(4):
            P.op("tensor", lambda t, _s=stt: t.transpose(
                tp0[0:8, 0:128], wtm[:, _s * 8:(_s + 1) * 8], t_id[:]),
                reads=[bwtm, b_id], writes=[btp0])
            P.op("vector", lambda v, _s=stt: v.tensor_copy(wfm8[:, _s * 128:(_s + 1) * 128], tp0[0:8, 0:128]),
                 reads=[btp0], writes=[bwfm])
        P.op("tensor", lambda t: t.matmul(tp1[:], t_e8[:], wfm8[:], start=True, stop=True),
             reads=[bwfm, b_e8], writes=[btp1])
        P.op("vector", lambda v: v.tensor_copy(wbc[:], tp1[:]), reads=[btp1], writes=[bwbc])

        def bh(t):
            r = None
            for c in range(NCK):
                lh = lat_h[:, c * 128:(c + 1) * 128]
                ll = lat_l[:, c * 128:(c + 1) * 128]
                rh = xnh[:, c * 512:(c + 1) * 512]
                rl = xnl[:, c * 512:(c + 1) * 512]
                t.matmul(pa[:], lh, rh, start=(c == 0), stop=False)
                t.matmul(pa[:], ll, rh, start=False, stop=False)
                r = t.matmul(pa[:], lh, rl, start=False, stop=(c == NCK - 1))
            return r
        P.op("tensor", bh, reads=[bxnh, bxnl, b_lat_h, b_lat_l], writes=[bpa])
        P.op("vector", lambda v: v.tensor_tensor(out=tb[:], in0=pa[:], in1=wbc[:], op=ALU.mult),
             reads=[bpa, bwbc], writes=[btb])
        split32(tb[:], btb, hph[:], hpl[:], bhph, bhpl, via_act=True)

    def stream_w_half(d, w_ap, o, half, slot):
        wf, bwf = d["wf32"][slot]
        wh, bwh = d["wp_h"][slot]
        wl, bwl = d["wp_l"][slot]
        c0 = half * 16
        src = w_ap[c0 * 128:(c0 + 16) * 128, o * 128:(o + 1) * 128].rearrange(
            "(c p) m -> p c m", p=128)
        P.dma("sync", wf[:].rearrange("p (c m) -> p c m", c=16), src, writes=[bwf])
        P.dma("gpsimd", wh[:], wf[:], reads=[bwf], writes=[bwh])
        P.op("vector", lambda v, _f=wf, _h=wh, _l=wl: v.tensor_tensor(
            out=_l[:], in0=_f[:], in1=_h[:], op=ALU.subtract),
            reads=[bwf, bwh], writes=[bwl])

    def mm_half(d, width, pm_pair, half, slot, start, stop, rh_pair=None, lora_o=None):
        if rh_pair is None:
            xnh, bxnh = d["xn_h"]
            xnl, bxnl = d["xn_l"]
        else:
            (xnh, bxnh), (xnl, bxnl) = rh_pair
        wh, bwh = d["wp_h"][slot]
        wl, bwl = d["wp_l"][slot]
        pmt, bpm = pm_pair
        hph, bhph = d["hp_h"]
        hpl, bhpl = d["hp_l"]
        c0 = half * 16

        def b(t, _w=width, _c0=c0, _st=start, _sp=stop, _lo=lora_o):
            r = None
            for k in range(16):
                c = _c0 + k
                lh = wh[:, k * 128:(k + 1) * 128]
                ll = wl[:, k * 128:(k + 1) * 128]
                rh = xnh[:, c * 512:c * 512 + _w]
                rl = xnl[:, c * 512:c * 512 + _w]
                t.matmul(pmt[:, 0:_w], lh, rh, start=(_st and k == 0), stop=False,
                         skip_group_check=True)
                t.matmul(pmt[:, 0:_w], ll, rh, start=False, stop=False,
                         skip_group_check=True)
                last = _sp and (_lo is None) and (k == 15)
                r = t.matmul(pmt[:, 0:_w], lh, rl, start=False, stop=last,
                             skip_group_check=True)
            if _sp and _lo is not None:
                lh = lbt_h[:, _lo * 128:(_lo + 1) * 128]
                ll = lbt_l[:, _lo * 128:(_lo + 1) * 128]
                t.matmul(pmt[:, 0:_w], lh, hph[:, 0:_w], start=False, stop=False,
                         skip_group_check=True)
                t.matmul(pmt[:, 0:_w], ll, hph[:, 0:_w], start=False, stop=False,
                         skip_group_check=True)
                r = t.matmul(pmt[:, 0:_w], lh, hpl[:, 0:_w], start=False, stop=True,
                             skip_group_check=True)
            return r
        reads = [bxnh, bxnl, bwh, bwl]
        if stop and lora_o is not None:
            reads += [bhph, bhpl, b_lbt_h, b_lbt_l]
        P.op("tensor", b, reads=reads, writes=[bpm])

    def mm_ochunk_mms(d, o, width, with_lora, rh_pair=None):
        pm_pair = d["pm"][o % 2]
        for half in range(2):
            mm_half(d, width, pm_pair, half, half, start=(half == 0), stop=(half == 1),
                    rh_pair=rh_pair, lora_o=(o if with_lora else None))
        return pm_pair

    def stream_o(d, w_ap, o):
        stream_w_half(d, w_ap, o, 0, 0)
        stream_w_half(d, w_ap, o, 1, 1)

    def pipelined_proj(d, w_ap, n_out, width, make_epilogue, with_lora=False, rh_pair=None):
        stream_o(d, w_ap, 0)
        pend = None
        for o in range(n_out):
            pm_pair = mm_ochunk_mms(d, o, width, with_lora, rh_pair=rh_pair)
            if pend is not None:
                pend()
            if o + 1 < n_out:
                stream_o(d, w_ap, o + 1)
            pend = (lambda _o=o, _pm=pm_pair: make_epilogue(_o, _pm[0], _pm[1]))
        pend()

    def ffn_base_loop(d, w_ap, resid_sc, bresid, resid_col0, out_cb, stats=True):
        tc, btc = d["tc"]
        td, btd = d["td"]

        def epi(o, pmt, bpm):
            xr, bxr = d["xr"][o % 2]
            P.dma("sync", xr[:], resid_sc[o, :, resid_col0:resid_col0 + 512],
                  reads=[bresid], writes=[bxr])
            P.op("vector", lambda v, _p=pmt, _o=o: v.tensor_scalar(
                out=tc[:], in0=_p[:], scalar1=bbc[:, _o:_o + 1], scalar2=0.5,
                op0=ALU.add, op1=ALU.mult), reads=[bpm, b_bbc], writes=[btc])
            P.op("vector", lambda v, _x=xr: v.tensor_tensor(
                out=td[:], in0=tc[:], in1=_x[:], op=ALU.add),
                reads=[btc, bxr], writes=[btd])
            if stats:
                stat_mms(d, td[:], btd, o == 0, o == NCK - 1)
            out_cb(o, td, btd)
        pipelined_proj(d, w_ap, NCK, 512, epi, with_lora=True)

    def rope_split_spill(d, o, pmt, bpm, cos_t, bcos, sin_t, bsin, col0,
                         dst_h, bdh, dst_l, bdl, dst_col0, norm_t, bnorm2):
        ta, bta = d["ta"]
        tb, btb = d["tb"]
        tc, btc = d["tc"]
        td, btd = d["td"]
        chh, bchh = d["ch_h"][o % 2]
        chl, bchl = d["ch_l"][o % 2]
        sq, bsq = d["sq32"]
        pn, bpn = d["pnorm"]
        P.op("vector", lambda v, _p=pmt: v.tensor_copy(ta[:], _p[:]), reads=[bpm], writes=[bta])
        P.dma("scalar", tb[0:64, :], ta[64:128, :], reads=[bta], writes=[btb])
        P.dma("scalar", tb[64:128, :], ta[0:64, :], reads=[bta], writes=[btb])
        P.op("vector", lambda v, _c0=col0: v.tensor_tensor(
            out=tc[:], in0=ta[:], in1=cos_t[:, _c0:_c0 + 512], op=ALU.mult),
            reads=[bta, bcos], writes=[btc])
        P.op("vector", lambda v, _c0=col0: v.tensor_tensor(
            out=td[:], in0=tb[:], in1=sin_t[:, _c0:_c0 + 512], op=ALU.mult),
            reads=[btb, bsin], writes=[btd])
        P.op("vector", lambda v: v.tensor_tensor(out=ta[:], in0=tc[:], in1=td[:], op=ALU.add),
             reads=[btc, btd], writes=[bta])
        P.op("scalar", lambda s: s.activation(out=sq[:], in_=ta[:], func=ACTF.Square),
             reads=[bta], writes=[bsq])
        P.op("tensor", lambda t: t.matmul(pn[:], ones_c[:], sq[:], start=True, stop=True),
             reads=[bsq, b_ones_c], writes=[bpn])
        P.op("vector", lambda v: v.tensor_copy(nrow[:], pn[:]), reads=[bpn], writes=[b_nrow])
        P.op("vector", lambda v, _n=norm_t: v.tensor_max(_n[:], _n[:], nrow[:]),
             reads=[b_nrow], writes=[bnorm2])
        split32(ta[:], bta, chh[:], chl[:], bchh, bchl, via_act=True)
        P.dma("scalar", dst_h[o, :, dst_col0:dst_col0 + 512], chh[:], reads=[bchh], writes=[bdh])
        P.dma("scalar", dst_l[o, :, dst_col0:dst_col0 + 512], chl[:], reads=[bchl], writes=[bdl])

    d = alloc_ffn_stack("a")
    stk1 = d["_stack"]

    for tt in range(4):
        t0 = tt * 512
        own_tt = tt >= 2
        j0 = (tt - 2) * 512

        x0, bx0 = d["x0tm"]
        ckt, b_ckt = d["ck"]
        skt, b_skt = d["sk"]
        P.dma("sync", ckt[:], cosk_d[:, t0:t0 + 512], writes=[b_ckt])
        P.dma("sync", skt[:], sink_d[:, t0:t0 + 512], writes=[b_skt])
        if own_tt:
            cqt, b_cqt = d["cq"]
            sqt, b_sqt = d["squ"]
            P.dma("sync", cqt[:], cosq_d[:, j0:j0 + 512], writes=[b_cqt])
            P.dma("sync", sqt[:], sinq_d[:, j0:j0 + 512], writes=[b_sqt])
        statc, bstat = d["statc"]
        sqtm, bsqtm = d["xn_h"]
        tc, btc = d["tc"]
        for stt in range(4):
            P.dma("sync", x0[:], xin[t0 + stt * 128: t0 + (stt + 1) * 128, :], writes=[bx0])
            P.op("vector", lambda v, _s=stt: v.reduce_sum(statc[:, _s:_s + 1], x0[:], axis=AX.X),
                 reads=[bx0], writes=[bstat])
            P.op("scalar", lambda s, _s=stt: s.activation(
                out=sqtm[:, 0:D].bitcast(BF16)[:, 0:D], in_=x0[:], func=ACTF.Square,
                accum_out=statc[:, 4 + _s:5 + _s]),
                reads=[bx0], writes=[bsqtm, bstat])
            P.op("scalar", lambda s, _s=stt: s.mul(
                out=statc[:, 8 + _s:9 + _s], in_=statc[:, _s:_s + 1], mul=INV_D),
                reads=[bstat], writes=[bstat])
            P.op("vector", lambda v, _s=stt: v.tensor_tensor(
                out=r1a[:], in0=statc[:, 8 + _s:9 + _s], in1=statc[:, 8 + _s:9 + _s], op=ALU.mult),
                reads=[bstat], writes=[b_r1a])
            P.op("vector", lambda v, _s=stt: v.scalar_tensor_tensor(
                out=r1b[:], in0=statc[:, 4 + _s:5 + _s], scalar=INV_D, in1=r1a[:],
                op0=ALU.mult, op1=ALU.subtract), reads=[bstat, b_r1a], writes=[b_r1b])
            P.op("scalar", lambda s: s.activation(out=r1b[:], in_=r1b[:], func=ACTF.Sqrt,
                                                  bias=epsc[:, 0:1]),
                 reads=[b_r1b, b_eps], writes=[b_r1b])
            P.op("vector", lambda v, _s=stt: v.reciprocal(statc[:, 12 + _s:13 + _s], r1b[:]),
                 reads=[b_r1b], writes=[bstat])
            P.op("vector", lambda v, _s=stt: v.tensor_tensor(
                out=statc[:, 8 + _s:9 + _s], in0=statc[:, 8 + _s:9 + _s],
                in1=statc[:, 12 + _s:13 + _s], op=ALU.mult), reads=[bstat], writes=[bstat])
            xds, bxds = d["xds"]
            for grp in range(8):
                tp, btp = d["ptp"][grp % 2]

                def btr(t, _g=grp, _tp=tp):
                    r = None
                    for k in range(4):
                        c = _g * 4 + k
                        r = t.transpose(_tp[:, k * 128:(k + 1) * 128],
                                        x0[:, c * 128:(c + 1) * 128], t_id[:])
                    return r
                P.op("tensor", btr, reads=[bx0, b_id], writes=[btp])
                P.op("vector", lambda v, _tp=tp, _g=grp: v.tensor_copy(
                    xds[:, (_g % 4) * 512:((_g % 4) + 1) * 512], _tp[:]),
                    reads=[btp], writes=[bxds])
                if grp % 4 == 3:
                    half = grp // 4
                    P.dma("gpsimd",
                          x0sc[half * 16:(half + 1) * 16, :, t0 + stt * 128: t0 + (stt + 1) * 128]
                          .rearrange("c p t -> p c t"),
                          xds[:].rearrange("p (c t) -> p c t", c=16),
                          reads=[bxds], writes=[b_x0sc])
        for stt in range(4):
            for bank, col, (dst, bdst) in ((0, 12 + stt, (rrow, b_rrow)),
                                           (1, 8 + stt, (mrow, b_mrow))):
                tp, btp = d["ptp"][bank]
                P.op("tensor", lambda t, _c=col, _tp=tp: t.transpose(
                    _tp[0:1, 0:128], statc[:, _c:_c + 1], t_id[:]),
                    reads=[bstat, b_id], writes=[btp])
                P.op("vector", lambda v, _tp=tp, _s=stt, _d=dst: v.tensor_copy(
                    _d[:, _s * 128:(_s + 1) * 128], _tp[0:1, 0:128]),
                    reads=[btp], writes=[bdst])
        bcast_rows(d)
        norm_split(d, x0sc, b_x0sc, t0)
        if debug and tt == 0:
            P.dma("gpsimd", dbg_statc[:, :], statc[:], reads=[bstat])
            P.dma("gpsimd", dbg_rrow[:, :], rrow[:], reads=[b_rrow])
            P.dma("gpsimd", dbg_mrow[:, :], mrow[:], reads=[b_mrow])
            P.dma("gpsimd", dbg_rbc[:, :], d["rbc"][0][:], reads=[d["rbc"][1]])
            P.dma("gpsimd", dbg_mrbc[:, :], d["mrbc"][0][:], reads=[d["mrbc"][1]])
            P.dma("gpsimd", dbg_xnh[:, :], d["xn_h"][0][:], reads=[d["xn_h"][1]])
            P.dma("gpsimd", dbg_xnl[:, :], d["xn_l"][0][:], reads=[d["xn_l"][1]])
        router_and_h(d)
        if debug and tt == 0:
            P.dma("gpsimd", dbg_hp[:, :], d["hp_h"][0][:], reads=[d["hp_h"][1]])
            P.dma("gpsimd", dbg_wbc[:, :], d["wbc"][0][:], reads=[d["wbc"][1]])

        def spill_x1(o, td_t, btd, _t0=t0):
            P.dma("scalar", x1sc[o, :, _t0:_t0 + 512], td_t[:], reads=[btd], writes=[b_x1sc])
        ffn_base_loop(d, wb_d, x0sc, b_x0sc, t0, spill_x1)
        ln_rows(d)
        bcast_rows(d)
        norm_split(d, x1sc, b_x1sc, t0)

        def k_epi(g, pmt, bpm, _t0=t0, _ck=ckt, _bck=b_ckt, _sk=skt, _bsk=b_skt):
            rope_split_spill(d, g, pmt, bpm, _ck, _bck, _sk, _bsk, 0,
                             khs, b_khs, kls, b_kls, _t0, nk2, b_nk2)
        pipelined_proj(d, wk_d, NKV, 512, k_epi)

        def v_epi(c, pmt, bpm, _tt=tt):
            ta, bta = d["ta"]
            tb, btb = d["tb"]
            chh, bchh = d["ch_h"][c % 2]
            chl, bchl = d["ch_l"][c % 2]
            tp0, btp0 = d["ptp"][0]
            P.op("vector", lambda v, _p=pmt: v.tensor_copy(ta[:], _p[:]), reads=[bpm], writes=[bta])

            def btr(t):
                r = None
                for sub in range(4):
                    r = t.transpose(tp0[:, sub * 128:(sub + 1) * 128],
                                    ta[:, sub * 128:(sub + 1) * 128], t_id[:])
                return r
            P.op("tensor", btr, reads=[bta, b_id], writes=[btp0])
            P.op("vector", lambda v: v.tensor_copy(tb[:], tp0[:]), reads=[btp0], writes=[btb])
            split32(tb[:], btb, chh[:], chl[:], bchh, bchl, via_act=True)
            P.dma("scalar",
                  vth[_tt * 4:(_tt + 1) * 4, :, c * 128:(c + 1) * 128].rearrange("T p d -> p T d"),
                  chh[:].rearrange("p (T d) -> p T d", T=4), reads=[bchh], writes=[b_vth])
            P.dma("scalar",
                  vtl[_tt * 4:(_tt + 1) * 4, :, c * 128:(c + 1) * 128].rearrange("T p d -> p T d"),
                  chl[:].rearrange("p (T d) -> p T d", T=4), reads=[bchl], writes=[b_vtl])
        pipelined_proj(d, wv_d, NKV, 512, v_epi)

        if own_tt:
            def q_epi(g, pmt, bpm, _j0=j0, _cq=cqt, _bcq=b_cqt, _sq=sqt, _bsq=b_sqt):
                rope_split_spill(d, g, pmt, bpm, _cq, _bcq, _sq, _bsq, 0,
                                 qhs, b_qhs, qls, b_qls, _j0, nq2, b_nq2)
            pipelined_proj(d, wq_d, NCK, 512, q_epi)

    P.flush()
    stk1.close()
    P.barrier()

    ph2 = ExitStack()
    mask_t, b_mask = SB(ph2, "mask_t", [128, NB * NST * QB], F32)
    kb_h = [SB(ph2, f"kb_h{i}", [128, NST * 128], BF16) for i in range(2)]
    kb_l = [SB(ph2, f"kb_l{i}", [128, NST * 128], BF16) for i in range(2)]
    vt_h = [SB(ph2, f"vt_h{i}", [128, NST * 128], BF16) for i in range(2)]
    vt_l = [SB(ph2, f"vt_l{i}", [128, NST * 128], BF16) for i in range(2)]
    qg_h = [SB(ph2, f"qg_h{i}", [128, 4 * QB], BF16) for i in range(2)]
    qg_l = [SB(ph2, f"qg_l{i}", [128, 4 * QB], BF16) for i in range(2)]
    aht, _ = SB(ph2, "a_h", [128, NST * QB], BF16)
    alt, _ = SB(ph2, "a_l", [128, NST * QB], BF16)
    b_ah = [Buf() for _ in range(NST)]
    b_al = [Buf() for _ in range(NST)]
    em = [SB(ph2, f"em{i}", [128, QB], F32) for i in range(2)]
    em2 = [SB(ph2, f"em2{i}", [128, QB], F32) for i in range(2)]
    rb_t, b_rb = SB(ph2, "rb_sb", [128, QB], F32)
    of_t, b_of = SB(ph2, "of32", [128, QB], F32)
    ohh, b_ohh = SB(ph2, "oh_sb", [128, QB], BF16)
    oll, b_oll = SB(ph2, "ol_sb", [128, QB], BF16)
    rr2, b_rr2 = SB(ph2, "rrow2", [1, QB], F32)
    pst = [PS(ph2, f"pst{i}", [128, QB]) for i in range(4)]
    pavt, b_pav = PS(ph2, "pav", [128, QB])
    psrt, b_psr = PS(ph2, "psr", [1, QB])
    pbct, b_pbc = PS(ph2, "pbc", [128, QB])

    P.op("vector", lambda v: v.reduce_max(r1a[0:1, 0:1], nq2[:], axis=AX.X),
         reads=[b_nq2], writes=[b_r1a])
    P.op("vector", lambda v: v.reduce_max(r1b[0:1, 0:1], nk2[:], axis=AX.X),
         reads=[b_nk2], writes=[b_r1b])
    P.op("vector", lambda v: v.tensor_tensor(out=r1a[0:1, 0:1], in0=r1a[0:1, 0:1],
                                             in1=r1b[0:1, 0:1], op=ALU.mult),
         reads=[b_r1b, b_r1a], writes=[b_r1a])
    P.op("vector", lambda v: v.tensor_scalar_mul(r1a[0:1, 0:1], r1a[0:1, 0:1], 1.0404),
         reads=[b_r1a], writes=[b_r1a])
    P.op("scalar", lambda s: s.activation(out=r1a[0:1, 0:1], in_=r1a[0:1, 0:1], func=ACTF.Sqrt),
         reads=[b_r1a], writes=[b_r1a])
    P.op("scalar", lambda s: s.mul(out=r1a[0:1, 0:1], in_=r1a[0:1, 0:1], mul=-1.0),
         reads=[b_r1a], writes=[b_r1a])
    P.op("tensor", lambda t: t.matmul(pbct[:, 0:1], ones_r[:], r1a[0:1, 0:1], start=True, stop=True),
         reads=[b_r1a, b_ones_r], writes=[b_pbc])
    P.op("vector", lambda v: v.tensor_copy(biasc[:], pbct[:, 0:1]), reads=[b_pbc], writes=[b_biasc])
    P.dma("sync", mask_t[:], masks_d[:, :], writes=[b_mask])

    for B in range(NB):
        s0 = 2 * B * 128
        for g in range(NKV):
            slot = g % 2
            kh, bkh = kb_h[slot]
            kl, bkl = kb_l[slot]
            vh, bvh = vt_h[slot]
            vl, bvl = vt_l[slot]
            qh_t, bqh = qg_h[slot]
            ql_t, bql = qg_l[slot]
            P.dma("sync", kh[:], khs[g, :, s0:s0 + NST * 128], reads=[b_khs], writes=[bkh])
            P.dma("sync", kl[:], kls[g, :, s0:s0 + NST * 128], reads=[b_kls], writes=[bkl])
            P.fill("sync", [(vh[:, st * 128:(st + 1) * 128],
                             vth[2 * B + st, :, g * 128:(g + 1) * 128]) for st in range(NST)],
                   reads=[b_vth], writes=[bvh])
            P.fill("sync", [(vl[:, st * 128:(st + 1) * 128],
                             vtl[2 * B + st, :, g * 128:(g + 1) * 128]) for st in range(NST)],
                   reads=[b_vtl], writes=[bvl])
            P.fill("sync", [(qh_t[:, j * QB:(j + 1) * QB],
                             qhs[g * 4 + j, :, B * QB:(B + 1) * QB]) for j in range(4)],
                   reads=[b_qhs], writes=[bqh])
            P.fill("sync", [(ql_t[:, j * QB:(j + 1) * QB],
                             qls[g * 4 + j, :, B * QB:(B + 1) * QB]) for j in range(4)],
                   reads=[b_qls], writes=[bql])
            for j in range(4):
                qh_ap = qh_t[:, j * QB:(j + 1) * QB]
                ql_ap = ql_t[:, j * QB:(j + 1) * QB]
                hq = g * 4 + j
                for step in range(NST + 2):
                    if step < NST:
                        st = step
                        ps_t, bps = pst[st % 4]

                        def bsc(t, _st=st, _ps=ps_t, _qh=qh_ap, _ql=ql_ap, _kh=kh, _kl=kl):
                            t.matmul(_ps[:], _kh[:, _st * 128:(_st + 1) * 128], _qh,
                                     start=True, stop=False)
                            t.matmul(_ps[:], _kl[:, _st * 128:(_st + 1) * 128], _qh,
                                     start=False, stop=False)
                            return t.matmul(_ps[:], _kh[:, _st * 128:(_st + 1) * 128], _ql,
                                            start=False, stop=True)
                        P.op("tensor", bsc, reads=[bkh, bkl, bqh, bql], writes=[bps])
                        emt, bem = em[st % 2]
                        em2t, bem2 = em2[st % 2]
                        P.op("scalar", lambda s, _p=ps_t, _e=emt: s.activation(
                            out=_e[:], in_=_p[:], func=ACTF.Exp, bias=biasc[:, 0:1]),
                            reads=[bps, b_biasc], writes=[bem])
                        mslice = mask_t[:, (B * NST + st) * QB:(B * NST + st + 1) * QB]
                        P.op("vector", lambda v, _e=emt, _m=mslice, _st=st: v.tensor_tensor(
                            out=aht[:, _st * QB:(_st + 1) * QB], in0=_e[:], in1=_m, op=ALU.mult),
                            reads=[bem, b_mask], writes=[b_ah[st]])
                        P.op("vector", lambda v, _e=emt, _st=st, _e2=em2t: v.tensor_tensor(
                            out=_e2[:], in0=_e[:], in1=aht[:, _st * QB:(_st + 1) * QB],
                            op=ALU.subtract),
                            reads=[bem, b_ah[st]], writes=[bem2])
                        P.op("vector", lambda v, _m=mslice, _st=st, _e2=em2t: v.tensor_tensor(
                            out=alt[:, _st * QB:(_st + 1) * QB], in0=_e2[:], in1=_m, op=ALU.mult),
                            reads=[bem2, b_mask], writes=[b_al[st]])
                    if step >= 2:
                        st = step - 2
                        def bsum(t, _st=st):
                            t.matmul(psrt[:], ones_b[:], aht[:, _st * QB:(_st + 1) * QB],
                                     start=(_st == 0), stop=False, skip_group_check=True)
                            return t.matmul(psrt[:], ones_b[:], alt[:, _st * QB:(_st + 1) * QB],
                                            start=False, stop=(_st == NST - 1),
                                            skip_group_check=True)
                        P.op("tensor", bsum,
                             reads=[b_ah[st], b_al[st], b_ones_b],
                             writes=([b_psr] if st == NST - 1 else []),
                             extra=(([b_psr.w] if b_psr.w else []) + list(b_psr.r) if st == 0 else []))

                def bav(t, _vh=vh, _vl=vl):
                    r = None
                    for st in range(NST):
                        ah_ap = aht[:, st * QB:(st + 1) * QB]
                        al_ap = alt[:, st * QB:(st + 1) * QB]
                        t.matmul(pavt[:], _vh[:, st * 128:(st + 1) * 128], ah_ap,
                                 start=(st == 0), stop=False, skip_group_check=True)
                        t.matmul(pavt[:], _vl[:, st * 128:(st + 1) * 128], ah_ap,
                                 start=False, stop=False, skip_group_check=True)
                        r = t.matmul(pavt[:], _vh[:, st * 128:(st + 1) * 128], al_ap,
                                     start=False, stop=(st == NST - 1), skip_group_check=True)
                    return r
                P.op("tensor", bav, reads=[bvh, bvl] + b_ah + b_al, writes=[b_pav])
                P.op("vector", lambda v: v.tensor_copy(rr2[:], psrt[:]),
                     reads=[b_psr], writes=[b_rr2])
                P.op("vector", lambda v: v.reciprocal(rr2[:], rr2[:]), reads=[b_rr2], writes=[b_rr2])
                P.op("tensor", lambda t: t.matmul(pbct[:], ones_r[:], rr2[:], start=True, stop=True),
                     reads=[b_rr2, b_ones_r], writes=[b_pbc])
                P.op("vector", lambda v: v.tensor_copy(rb_t[:], pbct[:]), reads=[b_pbc], writes=[b_rb])
                P.op("vector", lambda v: v.tensor_tensor(out=of_t[:], in0=pavt[:], in1=rb_t[:],
                                                         op=ALU.mult),
                     reads=[b_pav, b_rb], writes=[b_of])
                P.op("scalar", lambda s: s.copy(out=ohh[:], in_=of_t[:]), reads=[b_of], writes=[b_ohh])
                P.op("vector", lambda v: v.tensor_tensor(out=oll[:], in0=of_t[:], in1=ohh[:],
                                                         op=ALU.subtract),
                     reads=[b_of, b_ohh], writes=[b_oll])
                P.dma("gpsimd", osh[hq, :, B * QB:(B + 1) * QB], ohh[:], reads=[b_ohh], writes=[b_osh])
                P.dma("gpsimd", osl[hq, :, B * QB:(B + 1) * QB], oll[:], reads=[b_oll], writes=[b_osl])

    P.flush()
    ph2.close()
    P.barrier()

    d = alloc_ffn_stack("b")
    stk3 = d["_stack"]
    for tau in range(2):
        j0 = tau * 512
        col1 = 1024 + j0
        xnh, bxnh = d["xn_h"]
        xnl, bxnl = d["xn_l"]
        P.fill("sync", [(xnh[:, c * 512:(c + 1) * 512], osh[c, :, j0:j0 + 512])
                        for c in range(NCK)], reads=[b_osh], writes=[bxnh])
        P.fill("sync", [(xnl[:, c * 512:(c + 1) * 512], osl[c, :, j0:j0 + 512])
                        for c in range(NCK)], reads=[b_osl], writes=[bxnl])
        tc, btc = d["tc"]

        def o_epi(o, pmt, bpm, _j0=j0, _col1=col1):
            xr, bxr = d["xr"][o % 2]
            P.dma("sync", xr[:], x1sc[o, :, _col1:_col1 + 512], reads=[b_x1sc], writes=[bxr])
            P.op("vector", lambda v, _p=pmt, _x=xr: v.tensor_tensor(
                out=tc[:], in0=_p[:], in1=_x[:], op=ALU.add), reads=[bpm, bxr], writes=[btc])
            stat_mms(d, tc[:], btc, o == 0, o == NCK - 1)
            P.dma("scalar", x2sc[o, :, _j0:_j0 + 512], tc[:], reads=[btc], writes=[b_x2sc])
        pipelined_proj(d, wo_d, NCK, 512, o_epi)
        ln_rows(d)
        bcast_rows(d)
        norm_split(d, x2sc, b_x2sc, j0)
        router_and_h(d)

        def out_cb(o, td_t, btd, _j0=j0):
            ta, bta = d["ta"]
            for sub in range(4):
                tp, btp = d["ptp"][sub % 2]
                P.op("tensor", lambda t, _tp=tp, _s=sub, _c=td_t: t.transpose(
                    _tp[:, 0:128], _c[:, _s * 128:(_s + 1) * 128], t_id[:]),
                    reads=[btd, b_id], writes=[btp])
                P.op("vector", lambda v, _tp=tp, _s=sub: v.tensor_copy(
                    ta[:, _s * 128:(_s + 1) * 128], _tp[:, 0:128]),
                    reads=[btp], writes=[bta])
            for sub in range(4):
                P.dma("sync", yout[_j0 + sub * 128:_j0 + (sub + 1) * 128, o * 128:(o + 1) * 128],
                      ta[:, sub * 128:(sub + 1) * 128], reads=[bta])
        ffn_base_loop(d, wb_d, x2sc, b_x2sc, j0, out_cb, stats=False)

    P.flush()
    stk3.close()

    P._emit("gpsimd", lambda e: e.nop(),
            [("sy", P.cnt["sy"]), ("gq", P.cnt["gq"]), ("te", P.cnt["te"]),
             ("ve", P.cnt["ve"]), ("se", P.cnt["se"])], "gq", 16)
    P.flush()
    return nc


_NC_CACHE = None


def _host_tables(h):
    inv = 10000.0 ** (-np.arange(64, dtype=np.float64) / 64.0)
    sm = 1.0 / np.sqrt(128.0)

    def tables(pos, scale):
        ang = pos[None, :].astype(np.float64) * inv[:, None]
        cos = np.cos(ang) * scale
        sin = np.sin(ang) * scale
        cos_t = np.concatenate([cos, cos], axis=0).astype(np.float32)
        sin_t = np.concatenate([-sin, sin], axis=0).astype(np.float32)
        return np.ascontiguousarray(cos_t), np.ascontiguousarray(sin_t)

    pos_own = np.arange(OWN, dtype=np.float64) + h * OWN
    pos_ctx = (np.arange(CTX, dtype=np.float64) if h == 1
               else np.concatenate([np.arange(OWN), np.arange(OWN)]).astype(np.float64))
    cosq, sinq = tables(pos_own, sm)
    cosk, sink = tables(pos_ctx, 1.0)
    return cosq, sinq, cosk, sink


def _host_masks(h):
    m = np.zeros((NB * NST, 128, QB), dtype=np.float32)
    lo = 1024 if h == 0 else 0
    for B in range(NB):
        for st in range(NST):
            s_idx = (2 * B + st) * 128 + np.arange(128)[:, None]
            j_idx = B * QB + np.arange(QB)[None, :]
            ok = (s_idx > j_idx) & (s_idx <= j_idx + 1024) & (s_idx >= lo)
            m[B * NST + st] = ok.astype(np.float32)
    return np.ascontiguousarray(m.transpose(1, 0, 2).reshape(128, NB * NST * QB))


def kernel(L0_attn_x, L0_attn_wq, L0_attn_wk, L0_attn_wv, L0_attn_wo,
           L0_ffn_W_base, L0_ffn_b_base, L0_ffn_W_router, L0_ffn_lora_A, L0_ffn_lora_B):
    global _NC_CACHE
    x = np.ascontiguousarray(np.asarray(L0_attn_x, dtype=np.float32))
    wq = np.ascontiguousarray(np.asarray(L0_attn_wq, dtype=np.float32))
    wk = np.ascontiguousarray(np.asarray(L0_attn_wk, dtype=np.float32))
    wv = np.ascontiguousarray(np.asarray(L0_attn_wv, dtype=np.float32))
    wo = np.ascontiguousarray(np.asarray(L0_attn_wo, dtype=np.float32))
    wb = np.ascontiguousarray(np.asarray(L0_ffn_W_base, dtype=np.float32).T)
    bbv = np.asarray(L0_ffn_b_base, dtype=np.float32)
    bb = np.ascontiguousarray(bbv.reshape(NCK, 128).T)
    wrv = np.asarray(L0_ffn_W_router, dtype=np.float32).T
    wr = np.ascontiguousarray(wrv.reshape(NCK, 128, 8).transpose(1, 0, 2).reshape(128, NCK * 8))
    lav = np.asarray(L0_ffn_lora_A, dtype=np.float32).reshape(128, D).T
    la = np.ascontiguousarray(lav.reshape(NCK, 128, 128).transpose(1, 0, 2).reshape(128, NCK * 128))
    lb = np.ascontiguousarray(np.asarray(L0_ffn_lora_B, dtype=np.float32).transpose(0, 2, 1).reshape(128, D))
    idn = np.eye(128, dtype=np.float32)
    e8 = np.zeros((8, 128), dtype=np.float32)
    for e in range(8):
        e8[e, e * 16:(e + 1) * 16] = 2.0

    if _NC_CACHE is None:
        _NC_CACHE = build_program()
    nc = _NC_CACHE

    in_maps = _in_maps(x, wq, wk, wv, wo, wb, wr, la, lb, bb, idn, e8)
    res = run_bass_kernel_spmd(nc, in_maps, core_ids=list(range(8)))
    out = np.empty((4, CTX, D), dtype=np.float32)
    for i in range(8):
        b, h = i // 2, i % 2
        out[b, h * OWN:(h + 1) * OWN, :] = res.results[i]["yout"]
    return out


def _in_maps(x, wq, wk, wv, wo, wb, wr, la, lb, bb, idn, e8):
    in_maps = []
    for i in range(8):
        b, h = i // 2, i % 2
        xc = x[b] if h == 1 else np.concatenate([x[b, :OWN], x[b, :OWN]], axis=0)
        cosq, sinq, cosk, sink = _host_tables(h)
        in_maps.append({
            "xin": np.ascontiguousarray(xc), "wq": wq, "wk": wk, "wv": wv, "wo": wo,
            "wb": wb, "wr": wr, "la": la, "lb": lb, "bb": bb,
            "cosq": cosq, "sinq": sinq, "cosk": cosk, "sink": sink,
            "masks": _host_masks(h), "idn": idn, "e8": e8,
        })
    return in_maps


# version 09
# speedup vs baseline: 1.0207x; 1.0207x over previous
import numpy as np
import concourse.bass as bass
import concourse.mybir as mybir
from concourse.bass_utils import run_bass_kernel_spmd
from contextlib import ExitStack

F32 = mybir.dt.float32
BF16 = mybir.dt.bfloat16
ALU = mybir.AluOpType
ACTF = mybir.ActivationFunctionType
AX = mybir.AxisListType

D = 4096
CTX = 2048
OWN = 1024
NCK = 32
NKV = 8
NQH = 32
EPS = 1e-5
INV_D = 1.0 / D
QB = 256
NB = 4
NST = 10


class Buf:
    __slots__ = ("w", "r")

    def __init__(self):
        self.w = None
        self.r = []


class Prog:
    ENGINES = ("sync", "gpsimd", "tensor", "vector", "scalar")
    ESEM = {"sync": "sy", "gpsimd": "gq", "tensor": "te", "vector": "ve", "scalar": "se"}

    def __init__(self, nc, es):
        self.nc = nc
        self.es = es
        self.sems = {}
        self.cnt = {}
        self.q = {e: [] for e in self.ENGINES}
        self.last_wait = {e: {} for e in self.ENGINES}
        for name in ("sy", "gq", "te", "ve", "se", "bar"):
            self.sems[name] = es.enter_context(nc.semaphore(name))
            self.cnt[name] = 0
        self.nbar = 0

    def _emit(self, eng, build, waits, inc, inc_by):
        byname = {}
        for s, v in waits:
            if v is None or v <= 0:
                continue
            if s == self.ESEM[eng] and eng == "tensor":
                continue
            if v > byname.get(s, 0):
                byname[s] = v
        wl = []
        for s, v in byname.items():
            if self.last_wait[eng].get(s, 0) >= v:
                continue
            self.last_wait[eng][s] = v
            wl.append((self.sems[s], v))
        self.cnt[inc] += inc_by
        sem_inc = self.sems[inc]

        def run(e, _b=build, _w=wl, _s=sem_inc, _ib=inc_by):
            for sem, v in _w:
                e.wait_ge(sem, v)
            r = _b(e)
            r.then_inc(_s, _ib)

        self.q[eng].append(run)
        return (inc, self.cnt[inc])

    def op(self, eng, build, reads=(), writes=(), extra=(), dma=False):
        waits = list(extra)
        for b in reads:
            if b.w is not None:
                waits.append(b.w)
        for b in writes:
            if b.w is not None:
                waits.append(b.w)
            waits.extend(b.r)
        inc_by = 16 if dma else 1
        tok = self._emit(eng, build, waits, self.ESEM[eng], inc_by)
        for b in reads:
            b.r.append(tok)
        for b in writes:
            b.w = tok
            b.r = []
        return tok

    def fill(self, eng, pairs, reads=(), writes=()):
        waits = []
        for bf in reads:
            if bf.w is not None:
                waits.append(bf.w)
        for bf in writes:
            if bf.w is not None:
                waits.append(bf.w)
            waits.extend(bf.r)
        tok = None
        for i, (o_ap, i_ap) in enumerate(pairs):
            tok = self._emit(eng, (lambda e, _o=o_ap, _i=i_ap: e.dma_start(out=_o, in_=_i)),
                             (waits if i == 0 else []), self.ESEM[eng], 16)
        for bf in reads:
            bf.r.append(tok)
        for bf in writes:
            bf.w = tok
            bf.r = []
        return tok

    def dma(self, eng, out_ap, in_ap, reads=(), writes=(), extra=()):
        return self.op(eng, lambda e, _o=out_ap, _i=in_ap: e.dma_start(out=_o, in_=_i),
                       reads=reads, writes=writes, extra=extra, dma=True)

    def barrier(self):
        self.nbar += 1
        tgt = 5 * self.nbar
        for eng in self.ENGINES:
            extra = [(self.ESEM[eng], self.cnt[self.ESEM[eng]])]
            self._emit(eng, lambda e: e.nop(), extra, "bar", 1)
        for eng in self.ENGINES:
            self._emit(eng, lambda e: e.nop(), [("bar", tgt)], self.ESEM[eng],
                       16 if eng in ("sync", "gpsimd") else 1)

    def flush(self):
        with self.nc.Block() as block:
            for eng in self.ENGINES:
                ops = self.q[eng]
                if not ops:
                    continue

                def body(e, _ops=tuple(ops)):
                    for f in _ops:
                        f(e)

                getattr(block, eng)(body)
                self.q[eng] = []


def build_program(debug=False):
    nc = bass.Bass("TRN2", target_bir_lowering=False)
    DK = {"kind": "ExternalOutput"} if debug else {}

    xin = nc.dram_tensor("xin", [CTX, D], F32, kind="ExternalInput").ap()
    wq_d = nc.dram_tensor("wq", [D, D], F32, kind="ExternalInput").ap()
    wk_d = nc.dram_tensor("wk", [D, 1024], F32, kind="ExternalInput").ap()
    wv_d = nc.dram_tensor("wv", [D, 1024], F32, kind="ExternalInput").ap()
    wo_d = nc.dram_tensor("wo", [D, D], F32, kind="ExternalInput").ap()
    wb_d = nc.dram_tensor("wb", [D, D], F32, kind="ExternalInput").ap()
    wr_d = nc.dram_tensor("wr", [128, NCK * 8], F32, kind="ExternalInput").ap()
    la_d = nc.dram_tensor("la", [128, NCK * 128], F32, kind="ExternalInput").ap()
    lb_d = nc.dram_tensor("lb", [128, D], F32, kind="ExternalInput").ap()
    bb_d = nc.dram_tensor("bb", [128, NCK], F32, kind="ExternalInput").ap()
    cosq_d = nc.dram_tensor("cosq", [128, OWN], F32, kind="ExternalInput").ap()
    sinq_d = nc.dram_tensor("sinq", [128, OWN], F32, kind="ExternalInput").ap()
    cosk_d = nc.dram_tensor("cosk", [128, CTX], F32, kind="ExternalInput").ap()
    sink_d = nc.dram_tensor("sink", [128, CTX], F32, kind="ExternalInput").ap()
    masks_d = nc.dram_tensor("masks", [128, NB * NST * QB], F32, kind="ExternalInput").ap()
    idn_d = nc.dram_tensor("idn", [128, 128], F32, kind="ExternalInput").ap()
    e8_d = nc.dram_tensor("e8", [8, 128], F32, kind="ExternalInput").ap()
    yout = nc.dram_tensor("yout", [OWN, D], F32, kind="ExternalOutput").ap()

    x0sc = nc.dram_tensor("x0sc", [NCK, 128, CTX], F32, **DK).ap()
    x1sc = nc.dram_tensor("x1sc", [NCK, 128, CTX], F32, **DK).ap()
    x2sc = nc.dram_tensor("x2sc", [NCK, 128, OWN], F32, **DK).ap()
    khs = nc.dram_tensor("khs", [NKV, 128, CTX], BF16, **DK).ap()
    kls = nc.dram_tensor("kls", [NKV, 128, CTX], BF16, **DK).ap()
    vth = nc.dram_tensor("vth", [CTX // 128, 128, 1024], BF16, **DK).ap()
    vtl = nc.dram_tensor("vtl", [CTX // 128, 128, 1024], BF16, **DK).ap()
    qhs = nc.dram_tensor("qhs", [NQH, 128, OWN], BF16, **DK).ap()
    qls = nc.dram_tensor("qls", [NQH, 128, OWN], BF16, **DK).ap()
    osh = nc.dram_tensor("osh", [NCK, 128, OWN], BF16, **DK).ap()
    osl = nc.dram_tensor("osl", [NCK, 128, OWN], BF16, **DK).ap()
    if debug:
        dbg_statc = nc.dram_tensor("dbg_statc", [128, 16], F32, kind="ExternalOutput").ap()
        dbg_rrow = nc.dram_tensor("dbg_rrow", [1, 512], F32, kind="ExternalOutput").ap()
        dbg_mrow = nc.dram_tensor("dbg_mrow", [1, 512], F32, kind="ExternalOutput").ap()
        dbg_rbc = nc.dram_tensor("dbg_rbc", [128, 512], F32, kind="ExternalOutput").ap()
        dbg_mrbc = nc.dram_tensor("dbg_mrbc", [128, 512], F32, kind="ExternalOutput").ap()
        dbg_xnh = nc.dram_tensor("dbg_xnh", [128, NCK * 512], BF16, kind="ExternalOutput").ap()
        dbg_xnl = nc.dram_tensor("dbg_xnl", [128, NCK * 512], BF16, kind="ExternalOutput").ap()
        dbg_hp = nc.dram_tensor("dbg_hp", [128, 512], BF16, kind="ExternalOutput").ap()
        dbg_wbc = nc.dram_tensor("dbg_wbc", [128, 512], F32, kind="ExternalOutput").ap()

    es = ExitStack()
    P = Prog(nc, es)

    def SB(stack, name, shape, dt):
        t = stack.enter_context(nc.sbuf_tensor(name, shape, dt))
        return t, Buf()

    def PS(stack, name, shape):
        t = stack.enter_context(nc.psum_tensor(name, shape, F32))
        return t, Buf()

    lat_h, b_lat_h = SB(es, "lat_h", [128, NCK * 128], BF16)
    lat_l, b_lat_l = SB(es, "lat_l", [128, NCK * 128], BF16)
    lbt_h, b_lbt_h = SB(es, "lbt_h", [128, D], BF16)
    lbt_l, b_lbt_l = SB(es, "lbt_l", [128, D], BF16)
    wrt_h, b_wrt_h = SB(es, "wrt_h", [128, NCK * 8], BF16)
    wrt_l, b_wrt_l = SB(es, "wrt_l", [128, NCK * 8], BF16)
    t_id, b_id = SB(es, "t_id", [128, 128], F32)
    t_e8, b_e8 = SB(es, "t_e8", [8, 128], F32)
    ones_b, b_ones_b = SB(es, "ones_b", [128, 1], BF16)
    ones_c, b_ones_c = SB(es, "ones_c", [128, 1], F32)
    ones_r, b_ones_r = SB(es, "ones_r", [1, 128], F32)
    neg8, b_neg8 = SB(es, "neg8", [128, 8], F32)
    bbc, b_bbc = SB(es, "bbc", [128, NCK], F32)
    nq2, b_nq2 = SB(es, "nq2", [1, 512], F32)
    nk2, b_nk2 = SB(es, "nk2", [1, 512], F32)
    biasc, b_biasc = SB(es, "biasc", [128, 1], F32)
    rrow, b_rrow = SB(es, "rrow", [1, 512], F32)
    mrow, b_mrow = SB(es, "mrow", [1, 512], F32)
    srow, b_srow = SB(es, "srow", [1, 512], F32)
    ssrow, b_ssrow = SB(es, "ssrow", [1, 512], F32)
    nrow, b_nrow = SB(es, "nrow", [1, 512], F32)
    r1a, b_r1a = SB(es, "r1a", [128, 1], F32)
    epsc, b_eps = SB(es, "epsc", [128, 1], F32)
    r1b, b_r1b = SB(es, "r1b", [128, 1], F32)

    b_x0sc, b_x1sc, b_x2sc = Buf(), Buf(), Buf()
    b_khs, b_kls, b_vth, b_vtl = Buf(), Buf(), Buf(), Buf()
    b_qhs, b_qls, b_osh, b_osl = Buf(), Buf(), Buf(), Buf()

    P.dma("sync", t_id[:], idn_d[:, :], writes=[b_id])
    P.dma("sync", t_e8[:], e8_d[:, :], writes=[b_e8])
    P.dma("sync", bbc[:], bb_d[:, :], writes=[b_bbc])
    P.op("vector", lambda v: v.memset(ones_b[:], 1.0), writes=[b_ones_b])
    P.op("vector", lambda v: v.memset(ones_c[:], 1.0), writes=[b_ones_c])
    P.op("vector", lambda v: v.memset(ones_r[:], 1.0), writes=[b_ones_r])
    P.op("vector", lambda v: v.memset(neg8[:], -1e30), writes=[b_neg8])
    P.op("vector", lambda v: v.memset(epsc[:], EPS), writes=[b_eps])
    P.op("vector", lambda v: v.memset(nq2[:], 0.0), writes=[b_nq2])
    P.op("vector", lambda v: v.memset(nk2[:], 0.0), writes=[b_nk2])

    pre = ExitStack()
    pw32, b_pw32 = SB(pre, "pw32", [128, NCK * 128], F32)
    for dst_h, bh_, dst_l, bl_, src, ncols in (
            (lat_h, b_lat_h, lat_l, b_lat_l, la_d[:, :], NCK * 128),
            (lbt_h, b_lbt_h, lbt_l, b_lbt_l, lb_d[:, :], D),
            (wrt_h, b_wrt_h, wrt_l, b_wrt_l, wr_d[:, :], NCK * 8)):
        P.dma("sync", pw32[:, 0:ncols], src, writes=[b_pw32])
        P.dma("gpsimd", dst_h[:, 0:ncols], pw32[:, 0:ncols], reads=[b_pw32], writes=[bh_])
        P.op("vector", lambda v, _h=dst_h, _l=dst_l, _n=ncols: v.tensor_tensor(
            out=_l[:, 0:_n], in0=pw32[:, 0:_n], in1=_h[:, 0:_n], op=ALU.subtract),
            reads=[b_pw32, bh_], writes=[bl_])
    P.flush()
    pre.close()

    def alloc_ffn_stack(tag):
        ph = ExitStack()
        d = {"_stack": ph}
        d["x0tm"] = SB(ph, f"x0tm{tag}", [128, D], F32)
        d["rbc"] = SB(ph, f"rbc{tag}", [128, 512], F32)
        d["mrbc"] = SB(ph, f"mrbc{tag}", [128, 512], F32)
        d["xn_h"] = SB(ph, f"xn_h{tag}", [128, NCK * 512], BF16)
        d["xn_l"] = SB(ph, f"xn_l{tag}", [128, NCK * 512], BF16)
        d["wf32"] = [SB(ph, f"wf32{tag}{i}", [128, 16 * 128], F32) for i in range(2)]
        d["wp_h"] = [SB(ph, f"wp_h{tag}{i}", [128, 16 * 128], BF16) for i in range(2)]
        d["wp_l"] = [SB(ph, f"wp_l{tag}{i}", [128, 16 * 128], BF16) for i in range(2)]
        d["xr"] = [SB(ph, f"xr{tag}{i}", [128, 512], F32) for i in range(2)]
        d["ta"] = SB(ph, f"ta{tag}", [128, 512], F32)
        d["tb"] = SB(ph, f"tb{tag}", [128, 512], F32)
        d["tc"] = SB(ph, f"tc{tag}", [128, 512], F32)
        d["td"] = SB(ph, f"td{tag}", [128, 512], F32)
        d["sq32"] = SB(ph, f"sq32{tag}", [128, 512], F32)
        d["ch_h"] = [SB(ph, f"ch_h{tag}{i}", [128, 512], BF16) for i in range(2)]
        d["ch_l"] = [SB(ph, f"ch_l{tag}{i}", [128, 512], BF16) for i in range(2)]
        d["hp_h"] = SB(ph, f"hp_h{tag}", [128, 512], BF16)
        d["hp_l"] = SB(ph, f"hp_l{tag}", [128, 512], BF16)
        d["ltm"] = SB(ph, f"ltm{tag}", [128, 32], F32)
        d["wtm"] = SB(ph, f"wtm{tag}", [128, 32], F32)
        d["rt8a"] = SB(ph, f"rt8a{tag}", [128, 8], F32)
        d["rt8b"] = SB(ph, f"rt8b{tag}", [128, 8], F32)
        d["rt8m"] = SB(ph, f"rt8m{tag}", [128, 8], mybir.dt.uint8)
        d["rt8c"] = SB(ph, f"rt8c{tag}", [128, 8], F32)
        d["wfm8"] = SB(ph, f"wfm8{tag}", [8, 512], F32)
        d["wbc"] = SB(ph, f"wbc{tag}", [128, 512], F32)
        d["xds"] = SB(ph, f"xds{tag}", [128, 2048], F32)
        d["ck"] = SB(ph, f"ck{tag}", [128, 512], F32)
        d["sk"] = SB(ph, f"sk{tag}", [128, 512], F32)
        d["cq"] = SB(ph, f"cq{tag}", [128, 512], F32)
        d["squ"] = SB(ph, f"squ{tag}", [128, 512], F32)
        d["statc"] = SB(ph, f"statc{tag}", [128, 16], F32)
        d["pm"] = [PS(ph, f"pm{tag}{i}", [128, 512]) for i in range(2)]
        d["paux"] = PS(ph, f"paux{tag}", [128, 512])
        d["ptp"] = [PS(ph, f"ptp{tag}{i}", [128, 512]) for i in range(2)]
        d["prow0"] = PS(ph, f"prow0{tag}", [1, 512])
        d["prow1"] = PS(ph, f"prow1{tag}", [1, 512])
        d["pnorm"] = PS(ph, f"pnorm{tag}", [1, 512])
        return d

    def ln_rows(d):
        pr0, bpr0 = d["prow0"]
        pr1, bpr1 = d["prow1"]
        P.op("vector", lambda v: v.tensor_copy(srow[:], pr0[:]), reads=[bpr0], writes=[b_srow])
        P.op("vector", lambda v: v.tensor_copy(ssrow[:], pr1[:]), reads=[bpr1], writes=[b_ssrow])
        P.op("scalar", lambda s: s.mul(out=srow[:], in_=srow[:], mul=INV_D),
             reads=[b_srow], writes=[b_srow])
        P.op("vector", lambda v: v.tensor_tensor(out=mrow[:], in0=srow[:], in1=srow[:], op=ALU.mult),
             reads=[b_srow], writes=[b_mrow])
        P.op("vector", lambda v: v.scalar_tensor_tensor(
            out=ssrow[:], in0=ssrow[:], scalar=INV_D, in1=mrow[:], op0=ALU.mult, op1=ALU.subtract),
            reads=[b_ssrow, b_mrow], writes=[b_ssrow])
        P.op("scalar", lambda s: s.activation(out=ssrow[:], in_=ssrow[:], func=ACTF.Sqrt,
                                              bias=epsc[0:1, 0:1]),
             reads=[b_ssrow, b_eps], writes=[b_ssrow])
        P.op("vector", lambda v: v.reciprocal(rrow[:], ssrow[:]), reads=[b_ssrow], writes=[b_rrow])
        P.op("vector", lambda v: v.tensor_tensor(out=mrow[:], in0=srow[:], in1=rrow[:], op=ALU.mult),
             reads=[b_srow, b_rrow], writes=[b_mrow])

    def bcast_rows(d):
        for bank, (src, bsrc), (dst, bdst) in ((0, (rrow, b_rrow), d["rbc"]),
                                               (1, (mrow, b_mrow), d["mrbc"])):
            tp, btp = d["ptp"][bank]

            def b(t, _tp=tp, _s=src):
                r = None
                for qd in range(4):
                    r = t.matmul(_tp[:, qd * 128:(qd + 1) * 128], ones_r[:],
                                 _s[:, qd * 128:(qd + 1) * 128], start=(qd == 0), stop=(qd == 3))
                return r
            P.op("tensor", b, reads=[bsrc, b_ones_r], writes=[btp])
            P.op("vector", lambda v, _tp=tp, _d=dst: v.tensor_copy(_d[:], _tp[:]),
                 reads=[btp], writes=[bdst])

    def stat_mms(d, chunk_ap, bchunk, first, last):
        sq, bsq = d["sq32"]
        pr0, bpr0 = d["prow0"]
        pr1, bpr1 = d["prow1"]
        P.op("scalar", lambda s, _i=chunk_ap: s.activation(out=sq[:], in_=_i, func=ACTF.Square),
             reads=[bchunk], writes=[bsq])
        P.op("tensor", lambda t, _i=chunk_ap, _f=first, _l=last: t.matmul(
            pr0[:], ones_c[:], _i, start=_f, stop=_l, skip_group_check=True),
            reads=[bchunk, b_ones_c], writes=[bpr0])
        P.op("tensor", lambda t, _f=first, _l=last: t.matmul(
            pr1[:], ones_c[:], sq[:], start=_f, stop=_l, skip_group_check=True),
            reads=[bsq, b_ones_c], writes=[bpr1])

    def split32(src_ap, bsrc, dh_ap, dl_ap, bdst_h, bdst_l, via_act):
        if via_act:
            P.op("scalar", lambda s, _o=dh_ap, _i=src_ap: s.copy(out=_o, in_=_i),
                 reads=[bsrc], writes=[bdst_h])
        else:
            P.dma("gpsimd", dh_ap, src_ap, reads=[bsrc], writes=[bdst_h])
        P.op("vector", lambda v, _o=dl_ap, _i=src_ap, _h=dh_ap: v.tensor_tensor(
            out=_o, in0=_i, in1=_h, op=ALU.subtract), reads=[bsrc, bdst_h], writes=[bdst_l])

    def norm_split(d, src_sc, bsrc, col0):
        rbc, brbc = d["rbc"]
        mrbc, bmrbc = d["mrbc"]
        xnh, bxnh = d["xn_h"]
        xnl, bxnl = d["xn_l"]
        ta, bta = d["ta"]
        for c in range(NCK):
            xr, bxr = d["xr"][c % 2]
            P.dma("sync", xr[:], src_sc[c, :, col0:col0 + 512], reads=[bsrc], writes=[bxr])
            P.op("vector", lambda v, _x=xr: v.tensor_tensor(
                out=ta[:], in0=_x[:], in1=rbc[:], op=ALU.mult),
                reads=[bxr, brbc], writes=[bta])
            P.op("vector", lambda v: v.tensor_tensor(
                out=ta[:], in0=ta[:], in1=mrbc[:], op=ALU.subtract),
                reads=[bmrbc, bta], writes=[bta])
            split32(ta[:], bta, xnh[:, c * 512:(c + 1) * 512],
                    xnl[:, c * 512:(c + 1) * 512], bxnh, bxnl, via_act=(c % 2 == 0))

    def router_and_h(d):
        xnh, bxnh = d["xn_h"]
        xnl, bxnl = d["xn_l"]
        pa, bpa = d["paux"]
        tb, btb = d["tb"]
        ltm, bltm = d["ltm"]
        wtm, bwtm = d["wtm"]
        rt8a, b8a = d["rt8a"]
        rt8b, b8b = d["rt8b"]
        rt8m, b8m = d["rt8m"]
        rt8c, b8c = d["rt8c"]
        wfm8, bwfm = d["wfm8"]
        wbc, bwbc = d["wbc"]
        hph, bhph = d["hp_h"]
        hpl, bhpl = d["hp_l"]
        tp0, btp0 = d["ptp"][0]
        tp1, btp1 = d["ptp"][1]

        def blog(t):
            r = None
            for c in range(NCK):
                lh = wrt_h[:, c * 8:(c + 1) * 8]
                ll = wrt_l[:, c * 8:(c + 1) * 8]
                rh = xnh[:, c * 512:(c + 1) * 512]
                rl = xnl[:, c * 512:(c + 1) * 512]
                t.matmul(pa[0:8, :], lh, rh, start=(c == 0), stop=False)
                t.matmul(pa[0:8, :], ll, rh, start=False, stop=False)
                r = t.matmul(pa[0:8, :], lh, rl, start=False, stop=(c == NCK - 1))
            return r
        P.op("tensor", blog, reads=[bxnh, bxnl, b_wrt_h, b_wrt_l], writes=[bpa])
        P.op("vector", lambda v: v.tensor_copy(tb[0:8, :], pa[0:8, :]), reads=[bpa], writes=[btb])
        for stt in range(4):
            P.op("tensor", lambda t, _s=stt: t.transpose(
                tp0[0:128, 0:8], tb[0:8, _s * 128:(_s + 1) * 128], t_id[0:8, 0:8]),
                reads=[btb, b_id], writes=[btp0])
            P.op("vector", lambda v, _s=stt: v.tensor_copy(ltm[:, _s * 8:(_s + 1) * 8], tp0[0:128, 0:8]),
                 reads=[btp0], writes=[bltm])
        for stt in range(4):
            l8 = ltm[:, stt * 8:(stt + 1) * 8]
            w8 = wtm[:, stt * 8:(stt + 1) * 8]
            P.op("vector", lambda v, _l=l8: v.reduce_max(r1a[:], _l, axis=AX.X),
                 reads=[bltm], writes=[b_r1a])
            P.op("vector", lambda v, _l=l8: v.tensor_scalar(
                out=rt8a[:], in0=_l, scalar1=r1a[:, 0:1], scalar2=None, op0=ALU.subtract),
                reads=[bltm, b_r1a], writes=[b8a])
            P.op("vector", lambda v: v.tensor_single_scalar(
                out=rt8m[:], in_=rt8a[:], scalar=0.0, op=ALU.is_lt), reads=[b8a], writes=[b8m])
            P.op("vector", lambda v: v.select(out=rt8c[:], mask=rt8m[:], on_true=rt8a[:], on_false=neg8[:]),
                 reads=[b8a, b8m, b_neg8], writes=[b8c])
            P.op("vector", lambda v: v.reduce_max(r1b[:], rt8c[:], axis=AX.X),
                 reads=[b8c], writes=[b_r1b])
            P.op("vector", lambda v: v.tensor_scalar(
                out=rt8b[:], in0=rt8a[:], scalar1=r1b[:, 0:1], scalar2=None, op0=ALU.is_ge),
                reads=[b8a, b_r1b], writes=[b8b])
            P.op("scalar", lambda s: s.activation(out=r1a[:], in_=r1b[:], func=ACTF.Exp),
                 reads=[b_r1b], writes=[b_r1a])
            P.op("scalar", lambda s: s.activation(out=rt8a[:], in_=rt8a[:], func=ACTF.Exp),
                 reads=[b8a], writes=[b8a])
            P.op("vector", lambda v: v.tensor_scalar_add(r1a[:], r1a[:], 1.0),
                 reads=[b_r1a], writes=[b_r1a])
            P.op("vector", lambda v: v.reciprocal(r1a[:], r1a[:]), reads=[b_r1a], writes=[b_r1a])
            P.op("vector", lambda v, _w=w8: v.scalar_tensor_tensor(
                out=_w, in0=rt8a[:], scalar=r1a[:, 0:1], in1=rt8b[:], op0=ALU.mult, op1=ALU.mult),
                reads=[b8a, b_r1a, b8b], writes=[bwtm])
        for stt in range(4):
            P.op("tensor", lambda t, _s=stt: t.transpose(
                tp0[0:8, 0:128], wtm[:, _s * 8:(_s + 1) * 8], t_id[:]),
                reads=[bwtm, b_id], writes=[btp0])
            P.op("vector", lambda v, _s=stt: v.tensor_copy(wfm8[:, _s * 128:(_s + 1) * 128], tp0[0:8, 0:128]),
                 reads=[btp0], writes=[bwfm])
        P.op("tensor", lambda t: t.matmul(tp1[:], t_e8[:], wfm8[:], start=True, stop=True),
             reads=[bwfm, b_e8], writes=[btp1])
        P.op("vector", lambda v: v.tensor_copy(wbc[:], tp1[:]), reads=[btp1], writes=[bwbc])

        def bh(t):
            r = None
            for c in range(NCK):
                lh = lat_h[:, c * 128:(c + 1) * 128]
                ll = lat_l[:, c * 128:(c + 1) * 128]
                rh = xnh[:, c * 512:(c + 1) * 512]
                rl = xnl[:, c * 512:(c + 1) * 512]
                t.matmul(pa[:], lh, rh, start=(c == 0), stop=False)
                t.matmul(pa[:], ll, rh, start=False, stop=False)
                r = t.matmul(pa[:], lh, rl, start=False, stop=(c == NCK - 1))
            return r
        P.op("tensor", bh, reads=[bxnh, bxnl, b_lat_h, b_lat_l], writes=[bpa])
        P.op("vector", lambda v: v.tensor_tensor(out=tb[:], in0=pa[:], in1=wbc[:], op=ALU.mult),
             reads=[bpa, bwbc], writes=[btb])
        split32(tb[:], btb, hph[:], hpl[:], bhph, bhpl, via_act=True)

    def stream_w_half(d, w_ap, o, half, slot):
        wf, bwf = d["wf32"][slot]
        wh, bwh = d["wp_h"][slot]
        wl, bwl = d["wp_l"][slot]
        c0 = half * 16
        src = w_ap[c0 * 128:(c0 + 16) * 128, o * 128:(o + 1) * 128].rearrange(
            "(c p) m -> p c m", p=128)
        P.dma("sync", wf[:].rearrange("p (c m) -> p c m", c=16), src, writes=[bwf])
        P.dma("gpsimd", wh[:].rearrange("p (c m) -> p c m", c=16), src, writes=[bwh])
        P.op("vector", lambda v, _f=wf, _h=wh, _l=wl: v.tensor_tensor(
            out=_l[:], in0=_f[:], in1=_h[:], op=ALU.subtract),
            reads=[bwf, bwh], writes=[bwl])

    def mm_half(d, width, pm_pair, half, slot, start, stop, rh_pair=None, lora_o=None):
        if rh_pair is None:
            xnh, bxnh = d["xn_h"]
            xnl, bxnl = d["xn_l"]
        else:
            (xnh, bxnh), (xnl, bxnl) = rh_pair
        wh, bwh = d["wp_h"][slot]
        wl, bwl = d["wp_l"][slot]
        pmt, bpm = pm_pair
        hph, bhph = d["hp_h"]
        hpl, bhpl = d["hp_l"]
        c0 = half * 16

        def b(t, _w=width, _c0=c0, _st=start, _sp=stop, _lo=lora_o):
            r = None
            for k in range(16):
                c = _c0 + k
                lh = wh[:, k * 128:(k + 1) * 128]
                ll = wl[:, k * 128:(k + 1) * 128]
                rh = xnh[:, c * 512:c * 512 + _w]
                rl = xnl[:, c * 512:c * 512 + _w]
                t.matmul(pmt[:, 0:_w], lh, rh, start=(_st and k == 0), stop=False,
                         skip_group_check=True)
                t.matmul(pmt[:, 0:_w], ll, rh, start=False, stop=False,
                         skip_group_check=True)
                last = _sp and (_lo is None) and (k == 15)
                r = t.matmul(pmt[:, 0:_w], lh, rl, start=False, stop=last,
                             skip_group_check=True)
            if _sp and _lo is not None:
                lh = lbt_h[:, _lo * 128:(_lo + 1) * 128]
                ll = lbt_l[:, _lo * 128:(_lo + 1) * 128]
                t.matmul(pmt[:, 0:_w], lh, hph[:, 0:_w], start=False, stop=False,
                         skip_group_check=True)
                t.matmul(pmt[:, 0:_w], ll, hph[:, 0:_w], start=False, stop=False,
                         skip_group_check=True)
                r = t.matmul(pmt[:, 0:_w], lh, hpl[:, 0:_w], start=False, stop=True,
                             skip_group_check=True)
            return r
        reads = [bxnh, bxnl, bwh, bwl]
        if stop and lora_o is not None:
            reads += [bhph, bhpl, b_lbt_h, b_lbt_l]
        P.op("tensor", b, reads=reads, writes=[bpm])

    def mm_ochunk_mms(d, o, width, with_lora, rh_pair=None):
        pm_pair = d["pm"][o % 2]
        for half in range(2):
            mm_half(d, width, pm_pair, half, half, start=(half == 0), stop=(half == 1),
                    rh_pair=rh_pair, lora_o=(o if with_lora else None))
        return pm_pair

    def stream_o(d, w_ap, o):
        stream_w_half(d, w_ap, o, 0, 0)
        stream_w_half(d, w_ap, o, 1, 1)

    def pipelined_proj(d, w_ap, n_out, width, make_epilogue, with_lora=False, rh_pair=None):
        stream_o(d, w_ap, 0)
        pend = None
        for o in range(n_out):
            pm_pair = mm_ochunk_mms(d, o, width, with_lora, rh_pair=rh_pair)
            if pend is not None:
                pend()
            if o + 1 < n_out:
                stream_o(d, w_ap, o + 1)
            pend = (lambda _o=o, _pm=pm_pair: make_epilogue(_o, _pm[0], _pm[1]))
        pend()

    def ffn_base_loop(d, w_ap, resid_sc, bresid, resid_col0, out_cb, stats=True):
        tc, btc = d["tc"]
        td, btd = d["td"]

        def epi(o, pmt, bpm):
            xr, bxr = d["xr"][o % 2]
            P.dma("sync", xr[:], resid_sc[o, :, resid_col0:resid_col0 + 512],
                  reads=[bresid], writes=[bxr])
            P.op("vector", lambda v, _p=pmt, _o=o: v.tensor_scalar(
                out=tc[:], in0=_p[:], scalar1=bbc[:, _o:_o + 1], scalar2=0.5,
                op0=ALU.add, op1=ALU.mult), reads=[bpm, b_bbc], writes=[btc])
            P.op("vector", lambda v, _x=xr: v.tensor_tensor(
                out=td[:], in0=tc[:], in1=_x[:], op=ALU.add),
                reads=[btc, bxr], writes=[btd])
            if stats:
                stat_mms(d, td[:], btd, o == 0, o == NCK - 1)
            out_cb(o, td, btd)
        pipelined_proj(d, w_ap, NCK, 512, epi, with_lora=True)

    def rope_split_spill(d, o, pmt, bpm, cos_t, bcos, sin_t, bsin, col0,
                         dst_h, bdh, dst_l, bdl, dst_col0, norm_t, bnorm2):
        ta, bta = d["ta"]
        tb, btb = d["tb"]
        tc, btc = d["tc"]
        td, btd = d["td"]
        chh, bchh = d["ch_h"][o % 2]
        chl, bchl = d["ch_l"][o % 2]
        sq, bsq = d["sq32"]
        pn, bpn = d["pnorm"]
        P.op("vector", lambda v, _p=pmt: v.tensor_copy(ta[:], _p[:]), reads=[bpm], writes=[bta])
        P.dma("scalar", tb[0:64, :], ta[64:128, :], reads=[bta], writes=[btb])
        P.dma("scalar", tb[64:128, :], ta[0:64, :], reads=[bta], writes=[btb])
        P.op("vector", lambda v, _c0=col0: v.tensor_tensor(
            out=tc[:], in0=ta[:], in1=cos_t[:, _c0:_c0 + 512], op=ALU.mult),
            reads=[bta, bcos], writes=[btc])
        P.op("vector", lambda v, _c0=col0: v.tensor_tensor(
            out=td[:], in0=tb[:], in1=sin_t[:, _c0:_c0 + 512], op=ALU.mult),
            reads=[btb, bsin], writes=[btd])
        P.op("vector", lambda v: v.tensor_tensor(out=ta[:], in0=tc[:], in1=td[:], op=ALU.add),
             reads=[btc, btd], writes=[bta])
        P.op("scalar", lambda s: s.activation(out=sq[:], in_=ta[:], func=ACTF.Square),
             reads=[bta], writes=[bsq])
        P.op("tensor", lambda t: t.matmul(pn[:], ones_c[:], sq[:], start=True, stop=True),
             reads=[bsq, b_ones_c], writes=[bpn])
        P.op("vector", lambda v: v.tensor_copy(nrow[:], pn[:]), reads=[bpn], writes=[b_nrow])
        P.op("vector", lambda v, _n=norm_t: v.tensor_max(_n[:], _n[:], nrow[:]),
             reads=[b_nrow], writes=[bnorm2])
        split32(ta[:], bta, chh[:], chl[:], bchh, bchl, via_act=True)
        P.dma("scalar", dst_h[o, :, dst_col0:dst_col0 + 512], chh[:], reads=[bchh], writes=[bdh])
        P.dma("scalar", dst_l[o, :, dst_col0:dst_col0 + 512], chl[:], reads=[bchl], writes=[bdl])

    d = alloc_ffn_stack("a")
    stk1 = d["_stack"]

    for tt in range(4):
        t0 = tt * 512
        own_tt = tt >= 2
        j0 = (tt - 2) * 512

        x0, bx0 = d["x0tm"]
        ckt, b_ckt = d["ck"]
        skt, b_skt = d["sk"]
        P.dma("sync", ckt[:], cosk_d[:, t0:t0 + 512], writes=[b_ckt])
        P.dma("sync", skt[:], sink_d[:, t0:t0 + 512], writes=[b_skt])
        if own_tt:
            cqt, b_cqt = d["cq"]
            sqt, b_sqt = d["squ"]
            P.dma("sync", cqt[:], cosq_d[:, j0:j0 + 512], writes=[b_cqt])
            P.dma("sync", sqt[:], sinq_d[:, j0:j0 + 512], writes=[b_sqt])
        statc, bstat = d["statc"]
        sqtm, bsqtm = d["xn_h"]
        tc, btc = d["tc"]
        for stt in range(4):
            P.dma("sync", x0[:], xin[t0 + stt * 128: t0 + (stt + 1) * 128, :], writes=[bx0])
            P.op("vector", lambda v, _s=stt: v.reduce_sum(statc[:, _s:_s + 1], x0[:], axis=AX.X),
                 reads=[bx0], writes=[bstat])
            P.op("scalar", lambda s, _s=stt: s.activation(
                out=sqtm[:, 0:D].bitcast(BF16)[:, 0:D], in_=x0[:], func=ACTF.Square,
                accum_out=statc[:, 4 + _s:5 + _s]),
                reads=[bx0], writes=[bsqtm, bstat])
            P.op("scalar", lambda s, _s=stt: s.mul(
                out=statc[:, 8 + _s:9 + _s], in_=statc[:, _s:_s + 1], mul=INV_D),
                reads=[bstat], writes=[bstat])
            P.op("vector", lambda v, _s=stt: v.tensor_tensor(
                out=r1a[:], in0=statc[:, 8 + _s:9 + _s], in1=statc[:, 8 + _s:9 + _s], op=ALU.mult),
                reads=[bstat], writes=[b_r1a])
            P.op("vector", lambda v, _s=stt: v.scalar_tensor_tensor(
                out=r1b[:], in0=statc[:, 4 + _s:5 + _s], scalar=INV_D, in1=r1a[:],
                op0=ALU.mult, op1=ALU.subtract), reads=[bstat, b_r1a], writes=[b_r1b])
            P.op("scalar", lambda s: s.activation(out=r1b[:], in_=r1b[:], func=ACTF.Sqrt,
                                                  bias=epsc[:, 0:1]),
                 reads=[b_r1b, b_eps], writes=[b_r1b])
            P.op("vector", lambda v, _s=stt: v.reciprocal(statc[:, 12 + _s:13 + _s], r1b[:]),
                 reads=[b_r1b], writes=[bstat])
            P.op("vector", lambda v, _s=stt: v.tensor_tensor(
                out=statc[:, 8 + _s:9 + _s], in0=statc[:, 8 + _s:9 + _s],
                in1=statc[:, 12 + _s:13 + _s], op=ALU.mult), reads=[bstat], writes=[bstat])
            xds, bxds = d["xds"]
            for grp in range(8):
                tp, btp = d["ptp"][grp % 2]

                def btr(t, _g=grp, _tp=tp):
                    r = None
                    for k in range(4):
                        c = _g * 4 + k
                        r = t.transpose(_tp[:, k * 128:(k + 1) * 128],
                                        x0[:, c * 128:(c + 1) * 128], t_id[:])
                    return r
                P.op("tensor", btr, reads=[bx0, b_id], writes=[btp])
                P.op("vector", lambda v, _tp=tp, _g=grp: v.tensor_copy(
                    xds[:, (_g % 4) * 512:((_g % 4) + 1) * 512], _tp[:]),
                    reads=[btp], writes=[bxds])
                if grp % 4 == 3:
                    half = grp // 4
                    P.dma("gpsimd",
                          x0sc[half * 16:(half + 1) * 16, :, t0 + stt * 128: t0 + (stt + 1) * 128]
                          .rearrange("c p t -> p c t"),
                          xds[:].rearrange("p (c t) -> p c t", c=16),
                          reads=[bxds], writes=[b_x0sc])
        for stt in range(4):
            for bank, col, (dst, bdst) in ((0, 12 + stt, (rrow, b_rrow)),
                                           (1, 8 + stt, (mrow, b_mrow))):
                tp, btp = d["ptp"][bank]
                P.op("tensor", lambda t, _c=col, _tp=tp: t.transpose(
                    _tp[0:1, 0:128], statc[:, _c:_c + 1], t_id[:]),
                    reads=[bstat, b_id], writes=[btp])
                P.op("vector", lambda v, _tp=tp, _s=stt, _d=dst: v.tensor_copy(
                    _d[:, _s * 128:(_s + 1) * 128], _tp[0:1, 0:128]),
                    reads=[btp], writes=[bdst])
        bcast_rows(d)
        norm_split(d, x0sc, b_x0sc, t0)
        if debug and tt == 0:
            P.dma("gpsimd", dbg_statc[:, :], statc[:], reads=[bstat])
            P.dma("gpsimd", dbg_rrow[:, :], rrow[:], reads=[b_rrow])
            P.dma("gpsimd", dbg_mrow[:, :], mrow[:], reads=[b_mrow])
            P.dma("gpsimd", dbg_rbc[:, :], d["rbc"][0][:], reads=[d["rbc"][1]])
            P.dma("gpsimd", dbg_mrbc[:, :], d["mrbc"][0][:], reads=[d["mrbc"][1]])
            P.dma("gpsimd", dbg_xnh[:, :], d["xn_h"][0][:], reads=[d["xn_h"][1]])
            P.dma("gpsimd", dbg_xnl[:, :], d["xn_l"][0][:], reads=[d["xn_l"][1]])
        router_and_h(d)
        if debug and tt == 0:
            P.dma("gpsimd", dbg_hp[:, :], d["hp_h"][0][:], reads=[d["hp_h"][1]])
            P.dma("gpsimd", dbg_wbc[:, :], d["wbc"][0][:], reads=[d["wbc"][1]])

        def spill_x1(o, td_t, btd, _t0=t0):
            P.dma("scalar", x1sc[o, :, _t0:_t0 + 512], td_t[:], reads=[btd], writes=[b_x1sc])
        ffn_base_loop(d, wb_d, x0sc, b_x0sc, t0, spill_x1)
        ln_rows(d)
        bcast_rows(d)
        norm_split(d, x1sc, b_x1sc, t0)

        def k_epi(g, pmt, bpm, _t0=t0, _ck=ckt, _bck=b_ckt, _sk=skt, _bsk=b_skt):
            rope_split_spill(d, g, pmt, bpm, _ck, _bck, _sk, _bsk, 0,
                             khs, b_khs, kls, b_kls, _t0, nk2, b_nk2)
        pipelined_proj(d, wk_d, NKV, 512, k_epi)

        def v_epi(c, pmt, bpm, _tt=tt):
            ta, bta = d["ta"]
            tb, btb = d["tb"]
            chh, bchh = d["ch_h"][c % 2]
            chl, bchl = d["ch_l"][c % 2]
            tp0, btp0 = d["ptp"][0]
            P.op("vector", lambda v, _p=pmt: v.tensor_copy(ta[:], _p[:]), reads=[bpm], writes=[bta])

            def btr(t):
                r = None
                for sub in range(4):
                    r = t.transpose(tp0[:, sub * 128:(sub + 1) * 128],
                                    ta[:, sub * 128:(sub + 1) * 128], t_id[:])
                return r
            P.op("tensor", btr, reads=[bta, b_id], writes=[btp0])
            P.op("vector", lambda v: v.tensor_copy(tb[:], tp0[:]), reads=[btp0], writes=[btb])
            split32(tb[:], btb, chh[:], chl[:], bchh, bchl, via_act=True)
            P.dma("scalar",
                  vth[_tt * 4:(_tt + 1) * 4, :, c * 128:(c + 1) * 128].rearrange("T p d -> p T d"),
                  chh[:].rearrange("p (T d) -> p T d", T=4), reads=[bchh], writes=[b_vth])
            P.dma("scalar",
                  vtl[_tt * 4:(_tt + 1) * 4, :, c * 128:(c + 1) * 128].rearrange("T p d -> p T d"),
                  chl[:].rearrange("p (T d) -> p T d", T=4), reads=[bchl], writes=[b_vtl])
        pipelined_proj(d, wv_d, NKV, 512, v_epi)

        if own_tt:
            def q_epi(g, pmt, bpm, _j0=j0, _cq=cqt, _bcq=b_cqt, _sq=sqt, _bsq=b_sqt):
                rope_split_spill(d, g, pmt, bpm, _cq, _bcq, _sq, _bsq, 0,
                                 qhs, b_qhs, qls, b_qls, _j0, nq2, b_nq2)
            pipelined_proj(d, wq_d, NCK, 512, q_epi)

    P.flush()
    stk1.close()
    P.barrier()

    ph2 = ExitStack()
    mask_t, b_mask = SB(ph2, "mask_t", [128, NB * NST * QB], F32)
    kb_h = [SB(ph2, f"kb_h{i}", [128, NST * 128], BF16) for i in range(2)]
    kb_l = [SB(ph2, f"kb_l{i}", [128, NST * 128], BF16) for i in range(2)]
    vt_h = [SB(ph2, f"vt_h{i}", [128, NST * 128], BF16) for i in range(2)]
    vt_l = [SB(ph2, f"vt_l{i}", [128, NST * 128], BF16) for i in range(2)]
    qg_h = [SB(ph2, f"qg_h{i}", [128, 4 * QB], BF16) for i in range(2)]
    qg_l = [SB(ph2, f"qg_l{i}", [128, 4 * QB], BF16) for i in range(2)]
    aht, _ = SB(ph2, "a_h", [128, NST * QB], BF16)
    alt, _ = SB(ph2, "a_l", [128, NST * QB], BF16)
    b_ah = [Buf() for _ in range(NST)]
    b_al = [Buf() for _ in range(NST)]
    em = [SB(ph2, f"em{i}", [128, QB], F32) for i in range(2)]
    em2 = [SB(ph2, f"em2{i}", [128, QB], F32) for i in range(2)]
    rb_t, b_rb = SB(ph2, "rb_sb", [128, QB], F32)
    of_t, b_of = SB(ph2, "of32", [128, QB], F32)
    ohh, b_ohh = SB(ph2, "oh_sb", [128, QB], BF16)
    oll, b_oll = SB(ph2, "ol_sb", [128, QB], BF16)
    rr2, b_rr2 = SB(ph2, "rrow2", [1, QB], F32)
    pst = [PS(ph2, f"pst{i}", [128, QB]) for i in range(4)]
    pavt, b_pav = PS(ph2, "pav", [128, QB])
    psrt, b_psr = PS(ph2, "psr", [1, QB])
    pbct, b_pbc = PS(ph2, "pbc", [128, QB])

    P.op("vector", lambda v: v.reduce_max(r1a[0:1, 0:1], nq2[:], axis=AX.X),
         reads=[b_nq2], writes=[b_r1a])
    P.op("vector", lambda v: v.reduce_max(r1b[0:1, 0:1], nk2[:], axis=AX.X),
         reads=[b_nk2], writes=[b_r1b])
    P.op("vector", lambda v: v.tensor_tensor(out=r1a[0:1, 0:1], in0=r1a[0:1, 0:1],
                                             in1=r1b[0:1, 0:1], op=ALU.mult),
         reads=[b_r1b, b_r1a], writes=[b_r1a])
    P.op("vector", lambda v: v.tensor_scalar_mul(r1a[0:1, 0:1], r1a[0:1, 0:1], 1.0404),
         reads=[b_r1a], writes=[b_r1a])
    P.op("scalar", lambda s: s.activation(out=r1a[0:1, 0:1], in_=r1a[0:1, 0:1], func=ACTF.Sqrt),
         reads=[b_r1a], writes=[b_r1a])
    P.op("scalar", lambda s: s.mul(out=r1a[0:1, 0:1], in_=r1a[0:1, 0:1], mul=-1.0),
         reads=[b_r1a], writes=[b_r1a])
    P.op("tensor", lambda t: t.matmul(pbct[:, 0:1], ones_r[:], r1a[0:1, 0:1], start=True, stop=True),
         reads=[b_r1a, b_ones_r], writes=[b_pbc])
    P.op("vector", lambda v: v.tensor_copy(biasc[:], pbct[:, 0:1]), reads=[b_pbc], writes=[b_biasc])
    P.dma("sync", mask_t[:], masks_d[:, :], writes=[b_mask])

    for B in range(NB):
        s0 = 2 * B * 128
        for g in range(NKV):
            slot = g % 2
            kh, bkh = kb_h[slot]
            kl, bkl = kb_l[slot]
            vh, bvh = vt_h[slot]
            vl, bvl = vt_l[slot]
            qh_t, bqh = qg_h[slot]
            ql_t, bql = qg_l[slot]
            P.dma("sync", kh[:], khs[g, :, s0:s0 + NST * 128], reads=[b_khs], writes=[bkh])
            P.dma("sync", kl[:], kls[g, :, s0:s0 + NST * 128], reads=[b_kls], writes=[bkl])
            P.fill("sync", [(vh[:, st * 128:(st + 1) * 128],
                             vth[2 * B + st, :, g * 128:(g + 1) * 128]) for st in range(NST)],
                   reads=[b_vth], writes=[bvh])
            P.fill("sync", [(vl[:, st * 128:(st + 1) * 128],
                             vtl[2 * B + st, :, g * 128:(g + 1) * 128]) for st in range(NST)],
                   reads=[b_vtl], writes=[bvl])
            P.fill("sync", [(qh_t[:, j * QB:(j + 1) * QB],
                             qhs[g * 4 + j, :, B * QB:(B + 1) * QB]) for j in range(4)],
                   reads=[b_qhs], writes=[bqh])
            P.fill("sync", [(ql_t[:, j * QB:(j + 1) * QB],
                             qls[g * 4 + j, :, B * QB:(B + 1) * QB]) for j in range(4)],
                   reads=[b_qls], writes=[bql])
            for j in range(4):
                qh_ap = qh_t[:, j * QB:(j + 1) * QB]
                ql_ap = ql_t[:, j * QB:(j + 1) * QB]
                hq = g * 4 + j
                for step in range(NST + 2):
                    if step < NST:
                        st = step
                        ps_t, bps = pst[st % 4]

                        def bsc(t, _st=st, _ps=ps_t, _qh=qh_ap, _ql=ql_ap, _kh=kh, _kl=kl):
                            t.matmul(_ps[:], _kh[:, _st * 128:(_st + 1) * 128], _qh,
                                     start=True, stop=False)
                            t.matmul(_ps[:], _kl[:, _st * 128:(_st + 1) * 128], _qh,
                                     start=False, stop=False)
                            return t.matmul(_ps[:], _kh[:, _st * 128:(_st + 1) * 128], _ql,
                                            start=False, stop=True)
                        P.op("tensor", bsc, reads=[bkh, bkl, bqh, bql], writes=[bps])
                        emt, bem = em[st % 2]
                        em2t, bem2 = em2[st % 2]
                        P.op("scalar", lambda s, _p=ps_t, _e=emt: s.activation(
                            out=_e[:], in_=_p[:], func=ACTF.Exp, bias=biasc[:, 0:1]),
                            reads=[bps, b_biasc], writes=[bem])
                        mslice = mask_t[:, (B * NST + st) * QB:(B * NST + st + 1) * QB]
                        P.op("vector", lambda v, _e=emt, _m=mslice, _st=st: v.tensor_tensor(
                            out=aht[:, _st * QB:(_st + 1) * QB], in0=_e[:], in1=_m, op=ALU.mult),
                            reads=[bem, b_mask], writes=[b_ah[st]])
                        P.op("vector", lambda v, _e=emt, _st=st, _e2=em2t: v.tensor_tensor(
                            out=_e2[:], in0=_e[:], in1=aht[:, _st * QB:(_st + 1) * QB],
                            op=ALU.subtract),
                            reads=[bem, b_ah[st]], writes=[bem2])
                        P.op("vector", lambda v, _m=mslice, _st=st, _e2=em2t: v.tensor_tensor(
                            out=alt[:, _st * QB:(_st + 1) * QB], in0=_e2[:], in1=_m, op=ALU.mult),
                            reads=[bem2, b_mask], writes=[b_al[st]])
                    if step >= 2:
                        st = step - 2
                        def bsum(t, _st=st):
                            t.matmul(psrt[:], ones_b[:], aht[:, _st * QB:(_st + 1) * QB],
                                     start=(_st == 0), stop=False, skip_group_check=True)
                            return t.matmul(psrt[:], ones_b[:], alt[:, _st * QB:(_st + 1) * QB],
                                            start=False, stop=(_st == NST - 1),
                                            skip_group_check=True)
                        P.op("tensor", bsum,
                             reads=[b_ah[st], b_al[st], b_ones_b],
                             writes=([b_psr] if st == NST - 1 else []),
                             extra=(([b_psr.w] if b_psr.w else []) + list(b_psr.r) if st == 0 else []))

                def bav(t, _vh=vh, _vl=vl):
                    r = None
                    for st in range(NST):
                        ah_ap = aht[:, st * QB:(st + 1) * QB]
                        al_ap = alt[:, st * QB:(st + 1) * QB]
                        t.matmul(pavt[:], _vh[:, st * 128:(st + 1) * 128], ah_ap,
                                 start=(st == 0), stop=False, skip_group_check=True)
                        t.matmul(pavt[:], _vl[:, st * 128:(st + 1) * 128], ah_ap,
                                 start=False, stop=False, skip_group_check=True)
                        r = t.matmul(pavt[:], _vh[:, st * 128:(st + 1) * 128], al_ap,
                                     start=False, stop=(st == NST - 1), skip_group_check=True)
                    return r
                P.op("tensor", bav, reads=[bvh, bvl] + b_ah + b_al, writes=[b_pav])
                P.op("vector", lambda v: v.tensor_copy(rr2[:], psrt[:]),
                     reads=[b_psr], writes=[b_rr2])
                P.op("vector", lambda v: v.reciprocal(rr2[:], rr2[:]), reads=[b_rr2], writes=[b_rr2])
                P.op("tensor", lambda t: t.matmul(pbct[:], ones_r[:], rr2[:], start=True, stop=True),
                     reads=[b_rr2, b_ones_r], writes=[b_pbc])
                P.op("vector", lambda v: v.tensor_copy(rb_t[:], pbct[:]), reads=[b_pbc], writes=[b_rb])
                P.op("vector", lambda v: v.tensor_tensor(out=of_t[:], in0=pavt[:], in1=rb_t[:],
                                                         op=ALU.mult),
                     reads=[b_pav, b_rb], writes=[b_of])
                P.op("scalar", lambda s: s.copy(out=ohh[:], in_=of_t[:]), reads=[b_of], writes=[b_ohh])
                P.op("vector", lambda v: v.tensor_tensor(out=oll[:], in0=of_t[:], in1=ohh[:],
                                                         op=ALU.subtract),
                     reads=[b_of, b_ohh], writes=[b_oll])
                P.dma("gpsimd", osh[hq, :, B * QB:(B + 1) * QB], ohh[:], reads=[b_ohh], writes=[b_osh])
                P.dma("gpsimd", osl[hq, :, B * QB:(B + 1) * QB], oll[:], reads=[b_oll], writes=[b_osl])

    P.flush()
    ph2.close()
    P.barrier()

    d = alloc_ffn_stack("b")
    stk3 = d["_stack"]
    for tau in range(2):
        j0 = tau * 512
        col1 = 1024 + j0
        xnh, bxnh = d["xn_h"]
        xnl, bxnl = d["xn_l"]
        P.fill("sync", [(xnh[:, c * 512:(c + 1) * 512], osh[c, :, j0:j0 + 512])
                        for c in range(NCK)], reads=[b_osh], writes=[bxnh])
        P.fill("sync", [(xnl[:, c * 512:(c + 1) * 512], osl[c, :, j0:j0 + 512])
                        for c in range(NCK)], reads=[b_osl], writes=[bxnl])
        tc, btc = d["tc"]

        def o_epi(o, pmt, bpm, _j0=j0, _col1=col1):
            xr, bxr = d["xr"][o % 2]
            P.dma("sync", xr[:], x1sc[o, :, _col1:_col1 + 512], reads=[b_x1sc], writes=[bxr])
            P.op("vector", lambda v, _p=pmt, _x=xr: v.tensor_tensor(
                out=tc[:], in0=_p[:], in1=_x[:], op=ALU.add), reads=[bpm, bxr], writes=[btc])
            stat_mms(d, tc[:], btc, o == 0, o == NCK - 1)
            P.dma("scalar", x2sc[o, :, _j0:_j0 + 512], tc[:], reads=[btc], writes=[b_x2sc])
        pipelined_proj(d, wo_d, NCK, 512, o_epi)
        ln_rows(d)
        bcast_rows(d)
        norm_split(d, x2sc, b_x2sc, j0)
        router_and_h(d)

        def out_cb(o, td_t, btd, _j0=j0):
            ta, bta = d["ta"]
            for sub in range(4):
                tp, btp = d["ptp"][sub % 2]
                P.op("tensor", lambda t, _tp=tp, _s=sub, _c=td_t: t.transpose(
                    _tp[:, 0:128], _c[:, _s * 128:(_s + 1) * 128], t_id[:]),
                    reads=[btd, b_id], writes=[btp])
                P.op("vector", lambda v, _tp=tp, _s=sub: v.tensor_copy(
                    ta[:, _s * 128:(_s + 1) * 128], _tp[:, 0:128]),
                    reads=[btp], writes=[bta])
            for sub in range(4):
                P.dma("sync", yout[_j0 + sub * 128:_j0 + (sub + 1) * 128, o * 128:(o + 1) * 128],
                      ta[:, sub * 128:(sub + 1) * 128], reads=[bta])
        ffn_base_loop(d, wb_d, x2sc, b_x2sc, j0, out_cb, stats=False)

    P.flush()
    stk3.close()

    P._emit("gpsimd", lambda e: e.nop(),
            [("sy", P.cnt["sy"]), ("gq", P.cnt["gq"]), ("te", P.cnt["te"]),
             ("ve", P.cnt["ve"]), ("se", P.cnt["se"])], "gq", 16)
    P.flush()
    return nc


_NC_CACHE = None


def _host_tables(h):
    inv = 10000.0 ** (-np.arange(64, dtype=np.float64) / 64.0)
    sm = 1.0 / np.sqrt(128.0)

    def tables(pos, scale):
        ang = pos[None, :].astype(np.float64) * inv[:, None]
        cos = np.cos(ang) * scale
        sin = np.sin(ang) * scale
        cos_t = np.concatenate([cos, cos], axis=0).astype(np.float32)
        sin_t = np.concatenate([-sin, sin], axis=0).astype(np.float32)
        return np.ascontiguousarray(cos_t), np.ascontiguousarray(sin_t)

    pos_own = np.arange(OWN, dtype=np.float64) + h * OWN
    pos_ctx = (np.arange(CTX, dtype=np.float64) if h == 1
               else np.concatenate([np.arange(OWN), np.arange(OWN)]).astype(np.float64))
    cosq, sinq = tables(pos_own, sm)
    cosk, sink = tables(pos_ctx, 1.0)
    return cosq, sinq, cosk, sink


def _host_masks(h):
    m = np.zeros((NB * NST, 128, QB), dtype=np.float32)
    lo = 1024 if h == 0 else 0
    for B in range(NB):
        for st in range(NST):
            s_idx = (2 * B + st) * 128 + np.arange(128)[:, None]
            j_idx = B * QB + np.arange(QB)[None, :]
            ok = (s_idx > j_idx) & (s_idx <= j_idx + 1024) & (s_idx >= lo)
            m[B * NST + st] = ok.astype(np.float32)
    return np.ascontiguousarray(m.transpose(1, 0, 2).reshape(128, NB * NST * QB))


def kernel(L0_attn_x, L0_attn_wq, L0_attn_wk, L0_attn_wv, L0_attn_wo,
           L0_ffn_W_base, L0_ffn_b_base, L0_ffn_W_router, L0_ffn_lora_A, L0_ffn_lora_B):
    global _NC_CACHE
    x = np.ascontiguousarray(np.asarray(L0_attn_x, dtype=np.float32))
    wq = np.ascontiguousarray(np.asarray(L0_attn_wq, dtype=np.float32))
    wk = np.ascontiguousarray(np.asarray(L0_attn_wk, dtype=np.float32))
    wv = np.ascontiguousarray(np.asarray(L0_attn_wv, dtype=np.float32))
    wo = np.ascontiguousarray(np.asarray(L0_attn_wo, dtype=np.float32))
    wb = np.ascontiguousarray(np.asarray(L0_ffn_W_base, dtype=np.float32).T)
    bbv = np.asarray(L0_ffn_b_base, dtype=np.float32)
    bb = np.ascontiguousarray(bbv.reshape(NCK, 128).T)
    wrv = np.asarray(L0_ffn_W_router, dtype=np.float32).T
    wr = np.ascontiguousarray(wrv.reshape(NCK, 128, 8).transpose(1, 0, 2).reshape(128, NCK * 8))
    lav = np.asarray(L0_ffn_lora_A, dtype=np.float32).reshape(128, D).T
    la = np.ascontiguousarray(lav.reshape(NCK, 128, 128).transpose(1, 0, 2).reshape(128, NCK * 128))
    lb = np.ascontiguousarray(np.asarray(L0_ffn_lora_B, dtype=np.float32).transpose(0, 2, 1).reshape(128, D))
    idn = np.eye(128, dtype=np.float32)
    e8 = np.zeros((8, 128), dtype=np.float32)
    for e in range(8):
        e8[e, e * 16:(e + 1) * 16] = 2.0

    if _NC_CACHE is None:
        _NC_CACHE = build_program()
    nc = _NC_CACHE

    in_maps = _in_maps(x, wq, wk, wv, wo, wb, wr, la, lb, bb, idn, e8)
    res = run_bass_kernel_spmd(nc, in_maps, core_ids=list(range(8)))
    out = np.empty((4, CTX, D), dtype=np.float32)
    for i in range(8):
        b, h = i // 2, i % 2
        out[b, h * OWN:(h + 1) * OWN, :] = res.results[i]["yout"]
    return out


def _in_maps(x, wq, wk, wv, wo, wb, wr, la, lb, bb, idn, e8):
    in_maps = []
    for i in range(8):
        b, h = i // 2, i % 2
        xc = x[b] if h == 1 else np.concatenate([x[b, :OWN], x[b, :OWN]], axis=0)
        cosq, sinq, cosk, sink = _host_tables(h)
        in_maps.append({
            "xin": np.ascontiguousarray(xc), "wq": wq, "wk": wk, "wv": wv, "wo": wo,
            "wb": wb, "wr": wr, "la": la, "lb": lb, "bb": bb,
            "cosq": cosq, "sinq": sinq, "cosk": cosk, "sink": sink,
            "masks": _host_masks(h), "idn": idn, "e8": e8,
        })
    return in_maps


# version 10 (final)
# speedup vs baseline: 1.0794x; 1.0575x over previous
import numpy as np
import concourse.bass as bass
import concourse.mybir as mybir
from concourse.bass_utils import run_bass_kernel_spmd
from contextlib import ExitStack

F32 = mybir.dt.float32
BF16 = mybir.dt.bfloat16
ALU = mybir.AluOpType
ACTF = mybir.ActivationFunctionType
AX = mybir.AxisListType

D = 4096
CTX = 2048
OWN = 1024
NCK = 32
NKV = 8
NQH = 32
EPS = 1e-5
INV_D = 1.0 / D
QB = 256
NB = 4
NST = 10


class Buf:
    __slots__ = ("w", "r")

    def __init__(self):
        self.w = None
        self.r = []


class Prog:
    ENGINES = ("sync", "gpsimd", "tensor", "vector", "scalar")
    ESEM = {"sync": "sy", "gpsimd": "gq", "tensor": "te", "vector": "ve", "scalar": "se"}

    def __init__(self, nc, es):
        self.nc = nc
        self.es = es
        self.sems = {}
        self.cnt = {}
        self.q = {e: [] for e in self.ENGINES}
        self.last_wait = {e: {} for e in self.ENGINES}
        for name in ("sy", "gq", "te", "ve", "se", "bar"):
            self.sems[name] = es.enter_context(nc.semaphore(name))
            self.cnt[name] = 0
        self.nbar = 0

    def _emit(self, eng, build, waits, inc, inc_by):
        byname = {}
        for s, v in waits:
            if v is None or v <= 0:
                continue
            if s == self.ESEM[eng] and eng == "tensor":
                continue
            if v > byname.get(s, 0):
                byname[s] = v
        wl = []
        for s, v in byname.items():
            if self.last_wait[eng].get(s, 0) >= v:
                continue
            self.last_wait[eng][s] = v
            wl.append((self.sems[s], v))
        self.cnt[inc] += inc_by
        sem_inc = self.sems[inc]

        def run(e, _b=build, _w=wl, _s=sem_inc, _ib=inc_by):
            for sem, v in _w:
                e.wait_ge(sem, v)
            r = _b(e)
            r.then_inc(_s, _ib)

        self.q[eng].append(run)
        return (inc, self.cnt[inc])

    def op(self, eng, build, reads=(), writes=(), extra=(), dma=False):
        waits = list(extra)
        for b in reads:
            if b.w is not None:
                waits.append(b.w)
        for b in writes:
            if b.w is not None:
                waits.append(b.w)
            waits.extend(b.r)
        inc_by = 16 if dma else 1
        tok = self._emit(eng, build, waits, self.ESEM[eng], inc_by)
        for b in reads:
            b.r.append(tok)
        for b in writes:
            b.w = tok
            b.r = []
        return tok

    def fill(self, eng, pairs, reads=(), writes=()):
        waits = []
        for bf in reads:
            if bf.w is not None:
                waits.append(bf.w)
        for bf in writes:
            if bf.w is not None:
                waits.append(bf.w)
            waits.extend(bf.r)
        tok = None
        for i, (o_ap, i_ap) in enumerate(pairs):
            tok = self._emit(eng, (lambda e, _o=o_ap, _i=i_ap: e.dma_start(out=_o, in_=_i)),
                             (waits if i == 0 else []), self.ESEM[eng], 16)
        for bf in reads:
            bf.r.append(tok)
        for bf in writes:
            bf.w = tok
            bf.r = []
        return tok

    def dma(self, eng, out_ap, in_ap, reads=(), writes=(), extra=()):
        return self.op(eng, lambda e, _o=out_ap, _i=in_ap: e.dma_start(out=_o, in_=_i),
                       reads=reads, writes=writes, extra=extra, dma=True)

    def barrier(self):
        self.nbar += 1
        tgt = 5 * self.nbar
        for eng in self.ENGINES:
            extra = [(self.ESEM[eng], self.cnt[self.ESEM[eng]])]
            self._emit(eng, lambda e: e.nop(), extra, "bar", 1)
        for eng in self.ENGINES:
            self._emit(eng, lambda e: e.nop(), [("bar", tgt)], self.ESEM[eng],
                       16 if eng in ("sync", "gpsimd") else 1)

    def flush(self):
        with self.nc.Block() as block:
            for eng in self.ENGINES:
                ops = self.q[eng]
                if not ops:
                    continue

                def body(e, _ops=tuple(ops)):
                    for f in _ops:
                        f(e)

                getattr(block, eng)(body)
                self.q[eng] = []


def build_program(debug=False):
    nc = bass.Bass("TRN2", target_bir_lowering=False)
    DK = {"kind": "ExternalOutput"} if debug else {}

    xin = nc.dram_tensor("xin", [CTX, D], F32, kind="ExternalInput").ap()
    wq_d = nc.dram_tensor("wq", [D, D], F32, kind="ExternalInput").ap()
    wk_d = nc.dram_tensor("wk", [D, 1024], F32, kind="ExternalInput").ap()
    wv_d = nc.dram_tensor("wv", [D, 1024], F32, kind="ExternalInput").ap()
    wo_d = nc.dram_tensor("wo", [D, D], F32, kind="ExternalInput").ap()
    wb_d = nc.dram_tensor("wb", [D, D], F32, kind="ExternalInput").ap()
    wr_d = nc.dram_tensor("wr", [128, NCK * 8], F32, kind="ExternalInput").ap()
    la_d = nc.dram_tensor("la", [128, NCK * 128], F32, kind="ExternalInput").ap()
    lb_d = nc.dram_tensor("lb", [128, D], F32, kind="ExternalInput").ap()
    bb_d = nc.dram_tensor("bb", [128, NCK], F32, kind="ExternalInput").ap()
    cosq_d = nc.dram_tensor("cosq", [128, OWN], F32, kind="ExternalInput").ap()
    sinq_d = nc.dram_tensor("sinq", [128, OWN], F32, kind="ExternalInput").ap()
    cosk_d = nc.dram_tensor("cosk", [128, CTX], F32, kind="ExternalInput").ap()
    sink_d = nc.dram_tensor("sink", [128, CTX], F32, kind="ExternalInput").ap()
    masks_d = nc.dram_tensor("masks", [128, NB * NST * QB], F32, kind="ExternalInput").ap()
    idn_d = nc.dram_tensor("idn", [128, 128], F32, kind="ExternalInput").ap()
    e8_d = nc.dram_tensor("e8", [8, 128], F32, kind="ExternalInput").ap()
    yout = nc.dram_tensor("yout", [OWN, D], F32, kind="ExternalOutput").ap()

    x0sc = nc.dram_tensor("x0sc", [NCK, 128, CTX], F32, **DK).ap()
    x1sc = nc.dram_tensor("x1sc", [NCK, 128, CTX], F32, **DK).ap()
    x2sc = nc.dram_tensor("x2sc", [NCK, 128, OWN], F32, **DK).ap()
    khs = nc.dram_tensor("khs", [NKV, 128, CTX], BF16, **DK).ap()
    kls = nc.dram_tensor("kls", [NKV, 128, CTX], BF16, **DK).ap()
    vth = nc.dram_tensor("vth", [CTX // 128, 128, 1024], BF16, **DK).ap()
    vtl = nc.dram_tensor("vtl", [CTX // 128, 128, 1024], BF16, **DK).ap()
    qhs = nc.dram_tensor("qhs", [NQH, 128, OWN], BF16, **DK).ap()
    qls = nc.dram_tensor("qls", [NQH, 128, OWN], BF16, **DK).ap()
    osh = nc.dram_tensor("osh", [NCK, 128, OWN], BF16, **DK).ap()
    osl = nc.dram_tensor("osl", [NCK, 128, OWN], BF16, **DK).ap()
    if debug:
        dbg_statc = nc.dram_tensor("dbg_statc", [128, 16], F32, kind="ExternalOutput").ap()
        dbg_rrow = nc.dram_tensor("dbg_rrow", [1, 512], F32, kind="ExternalOutput").ap()
        dbg_mrow = nc.dram_tensor("dbg_mrow", [1, 512], F32, kind="ExternalOutput").ap()
        dbg_rbc = nc.dram_tensor("dbg_rbc", [128, 512], F32, kind="ExternalOutput").ap()
        dbg_mrbc = nc.dram_tensor("dbg_mrbc", [128, 512], F32, kind="ExternalOutput").ap()
        dbg_xnh = nc.dram_tensor("dbg_xnh", [128, NCK * 512], BF16, kind="ExternalOutput").ap()
        dbg_xnl = nc.dram_tensor("dbg_xnl", [128, NCK * 512], BF16, kind="ExternalOutput").ap()
        dbg_hp = nc.dram_tensor("dbg_hp", [128, 512], BF16, kind="ExternalOutput").ap()
        dbg_wbc = nc.dram_tensor("dbg_wbc", [128, 512], F32, kind="ExternalOutput").ap()

    es = ExitStack()
    P = Prog(nc, es)

    def SB(stack, name, shape, dt):
        t = stack.enter_context(nc.sbuf_tensor(name, shape, dt))
        return t, Buf()

    def PS(stack, name, shape):
        t = stack.enter_context(nc.psum_tensor(name, shape, F32))
        return t, Buf()

    lat_h, b_lat_h = SB(es, "lat_h", [128, NCK * 128], BF16)
    lat_l, b_lat_l = SB(es, "lat_l", [128, NCK * 128], BF16)
    lbt_h, b_lbt_h = SB(es, "lbt_h", [128, D], BF16)
    lbt_l, b_lbt_l = SB(es, "lbt_l", [128, D], BF16)
    wrt_h, b_wrt_h = SB(es, "wrt_h", [128, NCK * 8], BF16)
    wrt_l, b_wrt_l = SB(es, "wrt_l", [128, NCK * 8], BF16)
    t_id, b_id = SB(es, "t_id", [128, 128], F32)
    t_e8, b_e8 = SB(es, "t_e8", [8, 128], F32)
    ones_b, b_ones_b = SB(es, "ones_b", [128, 1], BF16)
    ones_c, b_ones_c = SB(es, "ones_c", [128, 1], F32)
    ones_r, b_ones_r = SB(es, "ones_r", [1, 128], F32)
    neg8, b_neg8 = SB(es, "neg8", [128, 8], F32)
    bbc, b_bbc = SB(es, "bbc", [128, NCK], F32)
    nq2, b_nq2 = SB(es, "nq2", [1, 512], F32)
    nk2, b_nk2 = SB(es, "nk2", [1, 512], F32)
    biasc, b_biasc = SB(es, "biasc", [128, 1], F32)
    rrow, b_rrow = SB(es, "rrow", [1, 512], F32)
    mrow, b_mrow = SB(es, "mrow", [1, 512], F32)
    srow, b_srow = SB(es, "srow", [1, 512], F32)
    ssrow, b_ssrow = SB(es, "ssrow", [1, 512], F32)
    nrow, b_nrow = SB(es, "nrow", [1, 512], F32)
    r1a, b_r1a = SB(es, "r1a", [128, 1], F32)
    epsc, b_eps = SB(es, "epsc", [128, 1], F32)
    r1b, b_r1b = SB(es, "r1b", [128, 1], F32)

    b_x0sc, b_x1sc, b_x2sc = Buf(), Buf(), Buf()
    b_khs, b_kls, b_vth, b_vtl = Buf(), Buf(), Buf(), Buf()
    b_qhs, b_qls, b_osh, b_osl = Buf(), Buf(), Buf(), Buf()

    P.dma("sync", t_id[:], idn_d[:, :], writes=[b_id])
    P.dma("sync", t_e8[:], e8_d[:, :], writes=[b_e8])
    P.dma("sync", bbc[:], bb_d[:, :], writes=[b_bbc])
    P.op("vector", lambda v: v.memset(ones_b[:], 1.0), writes=[b_ones_b])
    P.op("vector", lambda v: v.memset(ones_c[:], 1.0), writes=[b_ones_c])
    P.op("vector", lambda v: v.memset(ones_r[:], 1.0), writes=[b_ones_r])
    P.op("vector", lambda v: v.memset(neg8[:], -1e30), writes=[b_neg8])
    P.op("vector", lambda v: v.memset(epsc[:], EPS), writes=[b_eps])
    P.op("vector", lambda v: v.memset(nq2[:], 0.0), writes=[b_nq2])
    P.op("vector", lambda v: v.memset(nk2[:], 0.0), writes=[b_nk2])

    pre = ExitStack()
    pw32, b_pw32 = SB(pre, "pw32", [128, NCK * 128], F32)
    for dst_h, bh_, dst_l, bl_, src, ncols in (
            (lat_h, b_lat_h, lat_l, b_lat_l, la_d[:, :], NCK * 128),
            (lbt_h, b_lbt_h, lbt_l, b_lbt_l, lb_d[:, :], D),
            (wrt_h, b_wrt_h, wrt_l, b_wrt_l, wr_d[:, :], NCK * 8)):
        P.dma("sync", pw32[:, 0:ncols], src, writes=[b_pw32])
        P.op("scalar", lambda s, _o=dst_h, _i=pw32, _n=ncols: s.copy(out=_o[:, 0:_n], in_=_i[:, 0:_n]),
             reads=[b_pw32], writes=[bh_])
        P.op("vector", lambda v, _h=dst_h, _l=dst_l, _n=ncols: v.tensor_tensor(
            out=_l[:, 0:_n], in0=pw32[:, 0:_n], in1=_h[:, 0:_n], op=ALU.subtract),
            reads=[b_pw32, bh_], writes=[bl_])
    P.flush()
    pre.close()

    def alloc_ffn_stack(tag):
        ph = ExitStack()
        d = {"_stack": ph}
        d["x0tm"] = SB(ph, f"x0tm{tag}", [128, D], F32)
        d["rbc"] = SB(ph, f"rbc{tag}", [128, 512], F32)
        d["mrbc"] = SB(ph, f"mrbc{tag}", [128, 512], F32)
        d["xn_h"] = SB(ph, f"xn_h{tag}", [128, NCK * 512], BF16)
        d["xn_l"] = SB(ph, f"xn_l{tag}", [128, NCK * 512], BF16)
        d["wf32"] = [SB(ph, f"wf32{tag}{i}", [128, 16 * 128], F32) for i in range(2)]
        d["wp_h"] = [SB(ph, f"wp_h{tag}{i}", [128, 16 * 128], BF16) for i in range(3)]
        d["wp_l"] = [SB(ph, f"wp_l{tag}{i}", [128, 16 * 128], BF16) for i in range(3)]
        d["xr"] = [SB(ph, f"xr{tag}{i}", [128, 512], F32) for i in range(2)]
        d["ta"] = SB(ph, f"ta{tag}", [128, 512], F32)
        d["tb"] = SB(ph, f"tb{tag}", [128, 512], F32)
        d["tc"] = SB(ph, f"tc{tag}", [128, 512], F32)
        d["td"] = SB(ph, f"td{tag}", [128, 512], F32)
        d["sq32"] = SB(ph, f"sq32{tag}", [128, 512], F32)
        d["ch_h"] = [SB(ph, f"ch_h{tag}{i}", [128, 512], BF16) for i in range(2)]
        d["ch_l"] = [SB(ph, f"ch_l{tag}{i}", [128, 512], BF16) for i in range(2)]
        d["hp_h"] = SB(ph, f"hp_h{tag}", [128, 512], BF16)
        d["hp_l"] = SB(ph, f"hp_l{tag}", [128, 512], BF16)
        d["ltm"] = SB(ph, f"ltm{tag}", [128, 32], F32)
        d["wtm"] = SB(ph, f"wtm{tag}", [128, 32], F32)
        d["rt8a"] = SB(ph, f"rt8a{tag}", [128, 8], F32)
        d["rt8b"] = SB(ph, f"rt8b{tag}", [128, 8], F32)
        d["rt8m"] = SB(ph, f"rt8m{tag}", [128, 8], mybir.dt.uint8)
        d["rt8c"] = SB(ph, f"rt8c{tag}", [128, 8], F32)
        d["wfm8"] = SB(ph, f"wfm8{tag}", [8, 512], F32)
        d["wbc"] = SB(ph, f"wbc{tag}", [128, 512], F32)

        d["ck"] = SB(ph, f"ck{tag}", [128, 512], F32)
        d["sk"] = SB(ph, f"sk{tag}", [128, 512], F32)
        d["cq"] = SB(ph, f"cq{tag}", [128, 512], F32)
        d["squ"] = SB(ph, f"squ{tag}", [128, 512], F32)
        d["statc"] = SB(ph, f"statc{tag}", [128, 16], F32)
        d["pm"] = [PS(ph, f"pm{tag}{i}", [128, 512]) for i in range(2)]
        d["paux"] = PS(ph, f"paux{tag}", [128, 512])
        d["ptp"] = [PS(ph, f"ptp{tag}{i}", [128, 512]) for i in range(2)]
        d["prow0"] = PS(ph, f"prow0{tag}", [1, 512])
        d["prow1"] = PS(ph, f"prow1{tag}", [1, 512])
        d["pnorm"] = PS(ph, f"pnorm{tag}", [1, 512])
        return d

    def ln_rows(d):
        pr0, bpr0 = d["prow0"]
        pr1, bpr1 = d["prow1"]
        P.op("vector", lambda v: v.tensor_copy(srow[:], pr0[:]), reads=[bpr0], writes=[b_srow])
        P.op("vector", lambda v: v.tensor_copy(ssrow[:], pr1[:]), reads=[bpr1], writes=[b_ssrow])
        P.op("scalar", lambda s: s.mul(out=srow[:], in_=srow[:], mul=INV_D),
             reads=[b_srow], writes=[b_srow])
        P.op("vector", lambda v: v.tensor_tensor(out=mrow[:], in0=srow[:], in1=srow[:], op=ALU.mult),
             reads=[b_srow], writes=[b_mrow])
        P.op("vector", lambda v: v.scalar_tensor_tensor(
            out=ssrow[:], in0=ssrow[:], scalar=INV_D, in1=mrow[:], op0=ALU.mult, op1=ALU.subtract),
            reads=[b_ssrow, b_mrow], writes=[b_ssrow])
        P.op("scalar", lambda s: s.activation(out=ssrow[:], in_=ssrow[:], func=ACTF.Sqrt,
                                              bias=epsc[0:1, 0:1]),
             reads=[b_ssrow, b_eps], writes=[b_ssrow])
        P.op("vector", lambda v: v.reciprocal(rrow[:], ssrow[:]), reads=[b_ssrow], writes=[b_rrow])
        P.op("vector", lambda v: v.tensor_tensor(out=mrow[:], in0=srow[:], in1=rrow[:], op=ALU.mult),
             reads=[b_srow, b_rrow], writes=[b_mrow])

    def bcast_rows(d):
        for bank, (src, bsrc), (dst, bdst) in ((0, (rrow, b_rrow), d["rbc"]),
                                               (1, (mrow, b_mrow), d["mrbc"])):
            tp, btp = d["ptp"][bank]

            def b(t, _tp=tp, _s=src):
                r = None
                for qd in range(4):
                    r = t.matmul(_tp[:, qd * 128:(qd + 1) * 128], ones_r[:],
                                 _s[:, qd * 128:(qd + 1) * 128], start=(qd == 0), stop=(qd == 3))
                return r
            P.op("tensor", b, reads=[bsrc, b_ones_r], writes=[btp])
            P.op("vector", lambda v, _tp=tp, _d=dst: v.tensor_copy(_d[:], _tp[:]),
                 reads=[btp], writes=[bdst])

    def stat_mms(d, chunk_ap, bchunk, first, last):
        sq, bsq = d["sq32"]
        pr0, bpr0 = d["prow0"]
        pr1, bpr1 = d["prow1"]
        P.op("scalar", lambda s, _i=chunk_ap: s.activation(out=sq[:], in_=_i, func=ACTF.Square),
             reads=[bchunk], writes=[bsq])
        P.op("tensor", lambda t, _i=chunk_ap, _f=first, _l=last: t.matmul(
            pr0[:], ones_c[:], _i, start=_f, stop=_l, skip_group_check=True),
            reads=[bchunk, b_ones_c], writes=[bpr0])
        P.op("tensor", lambda t, _f=first, _l=last: t.matmul(
            pr1[:], ones_c[:], sq[:], start=_f, stop=_l, skip_group_check=True),
            reads=[bsq, b_ones_c], writes=[bpr1])

    def split32(src_ap, bsrc, dh_ap, dl_ap, bdst_h, bdst_l, via_act):
        P.op("scalar", lambda s, _o=dh_ap, _i=src_ap: s.copy(out=_o, in_=_i),
             reads=[bsrc], writes=[bdst_h])
        if False:
            pass
        P.op("vector", lambda v, _o=dl_ap, _i=src_ap, _h=dh_ap: v.tensor_tensor(
            out=_o, in0=_i, in1=_h, op=ALU.subtract), reads=[bsrc, bdst_h], writes=[bdst_l])

    def norm_split(d, src_sc, bsrc, col0):
        rbc, brbc = d["rbc"]
        mrbc, bmrbc = d["mrbc"]
        xnh, bxnh = d["xn_h"]
        xnl, bxnl = d["xn_l"]
        ta, bta = d["ta"]
        for c in range(NCK):
            xr, bxr = d["xr"][c % 2]
            P.dma("sync", xr[:], src_sc[c, :, col0:col0 + 512], reads=[bsrc], writes=[bxr])
            P.op("vector", lambda v, _x=xr: v.tensor_tensor(
                out=ta[:], in0=_x[:], in1=rbc[:], op=ALU.mult),
                reads=[bxr, brbc], writes=[bta])
            P.op("vector", lambda v: v.tensor_tensor(
                out=ta[:], in0=ta[:], in1=mrbc[:], op=ALU.subtract),
                reads=[bmrbc, bta], writes=[bta])
            split32(ta[:], bta, xnh[:, c * 512:(c + 1) * 512],
                    xnl[:, c * 512:(c + 1) * 512], bxnh, bxnl, via_act=(c % 2 == 0))

    def router_and_h(d):
        xnh, bxnh = d["xn_h"]
        xnl, bxnl = d["xn_l"]
        pa, bpa = d["paux"]
        tb, btb = d["tb"]
        ltm, bltm = d["ltm"]
        wtm, bwtm = d["wtm"]
        rt8a, b8a = d["rt8a"]
        rt8b, b8b = d["rt8b"]
        rt8m, b8m = d["rt8m"]
        rt8c, b8c = d["rt8c"]
        wfm8, bwfm = d["wfm8"]
        wbc, bwbc = d["wbc"]
        hph, bhph = d["hp_h"]
        hpl, bhpl = d["hp_l"]
        tp0, btp0 = d["ptp"][0]
        tp1, btp1 = d["ptp"][1]

        def blog(t):
            r = None
            for c in range(NCK):
                lh = wrt_h[:, c * 8:(c + 1) * 8]
                ll = wrt_l[:, c * 8:(c + 1) * 8]
                rh = xnh[:, c * 512:(c + 1) * 512]
                rl = xnl[:, c * 512:(c + 1) * 512]
                t.matmul(pa[0:8, :], lh, rh, start=(c == 0), stop=False)
                t.matmul(pa[0:8, :], ll, rh, start=False, stop=False)
                r = t.matmul(pa[0:8, :], lh, rl, start=False, stop=(c == NCK - 1))
            return r
        P.op("tensor", blog, reads=[bxnh, bxnl, b_wrt_h, b_wrt_l], writes=[bpa])
        P.op("vector", lambda v: v.tensor_copy(tb[0:8, :], pa[0:8, :]), reads=[bpa], writes=[btb])
        for stt in range(4):
            P.op("tensor", lambda t, _s=stt: t.transpose(
                tp0[0:128, 0:8], tb[0:8, _s * 128:(_s + 1) * 128], t_id[0:8, 0:8]),
                reads=[btb, b_id], writes=[btp0])
            P.op("vector", lambda v, _s=stt: v.tensor_copy(ltm[:, _s * 8:(_s + 1) * 8], tp0[0:128, 0:8]),
                 reads=[btp0], writes=[bltm])
        for stt in range(4):
            l8 = ltm[:, stt * 8:(stt + 1) * 8]
            w8 = wtm[:, stt * 8:(stt + 1) * 8]
            P.op("vector", lambda v, _l=l8: v.reduce_max(r1a[:], _l, axis=AX.X),
                 reads=[bltm], writes=[b_r1a])
            P.op("vector", lambda v, _l=l8: v.tensor_scalar(
                out=rt8a[:], in0=_l, scalar1=r1a[:, 0:1], scalar2=None, op0=ALU.subtract),
                reads=[bltm, b_r1a], writes=[b8a])
            P.op("vector", lambda v: v.tensor_single_scalar(
                out=rt8m[:], in_=rt8a[:], scalar=0.0, op=ALU.is_lt), reads=[b8a], writes=[b8m])
            P.op("vector", lambda v: v.select(out=rt8c[:], mask=rt8m[:], on_true=rt8a[:], on_false=neg8[:]),
                 reads=[b8a, b8m, b_neg8], writes=[b8c])
            P.op("vector", lambda v: v.reduce_max(r1b[:], rt8c[:], axis=AX.X),
                 reads=[b8c], writes=[b_r1b])
            P.op("vector", lambda v: v.tensor_scalar(
                out=rt8b[:], in0=rt8a[:], scalar1=r1b[:, 0:1], scalar2=None, op0=ALU.is_ge),
                reads=[b8a, b_r1b], writes=[b8b])
            P.op("scalar", lambda s: s.activation(out=r1a[:], in_=r1b[:], func=ACTF.Exp),
                 reads=[b_r1b], writes=[b_r1a])
            P.op("scalar", lambda s: s.activation(out=rt8a[:], in_=rt8a[:], func=ACTF.Exp),
                 reads=[b8a], writes=[b8a])
            P.op("vector", lambda v: v.tensor_scalar_add(r1a[:], r1a[:], 1.0),
                 reads=[b_r1a], writes=[b_r1a])
            P.op("vector", lambda v: v.reciprocal(r1a[:], r1a[:]), reads=[b_r1a], writes=[b_r1a])
            P.op("vector", lambda v, _w=w8: v.scalar_tensor_tensor(
                out=_w, in0=rt8a[:], scalar=r1a[:, 0:1], in1=rt8b[:], op0=ALU.mult, op1=ALU.mult),
                reads=[b8a, b_r1a, b8b], writes=[bwtm])
        for stt in range(4):
            P.op("tensor", lambda t, _s=stt: t.transpose(
                tp0[0:8, 0:128], wtm[:, _s * 8:(_s + 1) * 8], t_id[:]),
                reads=[bwtm, b_id], writes=[btp0])
            P.op("vector", lambda v, _s=stt: v.tensor_copy(wfm8[:, _s * 128:(_s + 1) * 128], tp0[0:8, 0:128]),
                 reads=[btp0], writes=[bwfm])
        P.op("tensor", lambda t: t.matmul(tp1[:], t_e8[:], wfm8[:], start=True, stop=True),
             reads=[bwfm, b_e8], writes=[btp1])
        P.op("vector", lambda v: v.tensor_copy(wbc[:], tp1[:]), reads=[btp1], writes=[bwbc])

        def bh(t):
            r = None
            for c in range(NCK):
                lh = lat_h[:, c * 128:(c + 1) * 128]
                ll = lat_l[:, c * 128:(c + 1) * 128]
                rh = xnh[:, c * 512:(c + 1) * 512]
                rl = xnl[:, c * 512:(c + 1) * 512]
                t.matmul(pa[:], lh, rh, start=(c == 0), stop=False)
                t.matmul(pa[:], ll, rh, start=False, stop=False)
                r = t.matmul(pa[:], lh, rl, start=False, stop=(c == NCK - 1))
            return r
        P.op("tensor", bh, reads=[bxnh, bxnl, b_lat_h, b_lat_l], writes=[bpa])
        P.op("vector", lambda v: v.tensor_tensor(out=tb[:], in0=pa[:], in1=wbc[:], op=ALU.mult),
             reads=[bpa, bwbc], writes=[btb])
        split32(tb[:], btb, hph[:], hpl[:], bhph, bhpl, via_act=True)

    def stream_w_half(d, w_ap, o, half, slot):
        wf, bwf = d["wf32"][(2 * o + half) % 2]
        wh, bwh = d["wp_h"][slot]
        wl, bwl = d["wp_l"][slot]
        c0 = half * 16
        src = w_ap[c0 * 128:(c0 + 16) * 128, o * 128:(o + 1) * 128].rearrange(
            "(c p) m -> p c m", p=128)
        P.dma("sync", wf[:].rearrange("p (c m) -> p c m", c=16), src, writes=[bwf])
        P.op("scalar", lambda s, _f=wf, _h=wh: s.copy(out=_h[:], in_=_f[:]),
             reads=[bwf], writes=[bwh])
        P.op("vector", lambda v, _f=wf, _h=wh, _l=wl: v.tensor_tensor(
            out=_l[:], in0=_f[:], in1=_h[:], op=ALU.subtract),
            reads=[bwf, bwh], writes=[bwl])

    def mm_half(d, width, pm_pair, half, slot, start, stop, rh_pair=None, lora_o=None):
        if rh_pair is None:
            xnh, bxnh = d["xn_h"]
            xnl, bxnl = d["xn_l"]
        else:
            (xnh, bxnh), (xnl, bxnl) = rh_pair
        wh, bwh = d["wp_h"][slot]
        wl, bwl = d["wp_l"][slot]
        pmt, bpm = pm_pair
        hph, bhph = d["hp_h"]
        hpl, bhpl = d["hp_l"]
        c0 = half * 16

        def b(t, _w=width, _c0=c0, _st=start, _sp=stop, _lo=lora_o):
            r = None
            for k in range(16):
                c = _c0 + k
                lh = wh[:, k * 128:(k + 1) * 128]
                ll = wl[:, k * 128:(k + 1) * 128]
                rh = xnh[:, c * 512:c * 512 + _w]
                rl = xnl[:, c * 512:c * 512 + _w]
                t.matmul(pmt[:, 0:_w], lh, rh, start=(_st and k == 0), stop=False,
                         skip_group_check=True)
                t.matmul(pmt[:, 0:_w], ll, rh, start=False, stop=False,
                         skip_group_check=True)
                last = _sp and (_lo is None) and (k == 15)
                r = t.matmul(pmt[:, 0:_w], lh, rl, start=False, stop=last,
                             skip_group_check=True)
            if _sp and _lo is not None:
                lh = lbt_h[:, _lo * 128:(_lo + 1) * 128]
                ll = lbt_l[:, _lo * 128:(_lo + 1) * 128]
                t.matmul(pmt[:, 0:_w], lh, hph[:, 0:_w], start=False, stop=False,
                         skip_group_check=True)
                t.matmul(pmt[:, 0:_w], ll, hph[:, 0:_w], start=False, stop=False,
                         skip_group_check=True)
                r = t.matmul(pmt[:, 0:_w], lh, hpl[:, 0:_w], start=False, stop=True,
                             skip_group_check=True)
            return r
        reads = [bxnh, bxnl, bwh, bwl]
        if stop and lora_o is not None:
            reads += [bhph, bhpl, b_lbt_h, b_lbt_l]
        P.op("tensor", b, reads=reads, writes=[bpm])

    def mm_ochunk_mms(d, o, width, with_lora, rh_pair=None):
        pm_pair = d["pm"][o % 2]
        for half in range(2):
            mm_half(d, width, pm_pair, half, (2 * o + half) % 3, start=(half == 0),
                    stop=(half == 1), rh_pair=rh_pair, lora_o=(o if with_lora else None))
        return pm_pair

    def stream_o(d, w_ap, o):
        stream_w_half(d, w_ap, o, 0, (2 * o) % 3)
        stream_w_half(d, w_ap, o, 1, (2 * o + 1) % 3)

    def pipelined_proj(d, w_ap, n_out, width, make_epilogue, with_lora=False, rh_pair=None):
        stream_o(d, w_ap, 0)
        pend = None
        for o in range(n_out):
            pm_pair = mm_ochunk_mms(d, o, width, with_lora, rh_pair=rh_pair)
            if pend is not None:
                pend()
            if o + 1 < n_out:
                stream_o(d, w_ap, o + 1)
            pend = (lambda _o=o, _pm=pm_pair: make_epilogue(_o, _pm[0], _pm[1]))
        pend()

    def ffn_base_loop(d, w_ap, resid_sc, bresid, resid_col0, out_cb, stats=True):
        tc, btc = d["tc"]
        td, btd = d["td"]

        def epi(o, pmt, bpm):
            xr, bxr = d["xr"][o % 2]
            P.dma("sync", xr[:], resid_sc[o, :, resid_col0:resid_col0 + 512],
                  reads=[bresid], writes=[bxr])
            P.op("vector", lambda v, _p=pmt, _o=o: v.tensor_scalar(
                out=tc[:], in0=_p[:], scalar1=bbc[:, _o:_o + 1], scalar2=0.5,
                op0=ALU.add, op1=ALU.mult), reads=[bpm, b_bbc], writes=[btc])
            P.op("vector", lambda v, _x=xr: v.tensor_tensor(
                out=td[:], in0=tc[:], in1=_x[:], op=ALU.add),
                reads=[btc, bxr], writes=[btd])
            if stats:
                stat_mms(d, td[:], btd, o == 0, o == NCK - 1)
            out_cb(o, td, btd)
        pipelined_proj(d, w_ap, NCK, 512, epi, with_lora=True)

    def rope_split_spill(d, o, pmt, bpm, cos_t, bcos, sin_t, bsin, col0,
                         dst_h, bdh, dst_l, bdl, dst_col0, norm_t, bnorm2):
        ta, bta = d["ta"]
        tb, btb = d["tb"]
        tc, btc = d["tc"]
        td, btd = d["td"]
        chh, bchh = d["ch_h"][o % 2]
        chl, bchl = d["ch_l"][o % 2]
        sq, bsq = d["sq32"]
        pn, bpn = d["pnorm"]
        P.op("vector", lambda v, _p=pmt: v.tensor_copy(ta[:], _p[:]), reads=[bpm], writes=[bta])
        P.dma("scalar", tb[0:64, :], ta[64:128, :], reads=[bta], writes=[btb])
        P.dma("scalar", tb[64:128, :], ta[0:64, :], reads=[bta], writes=[btb])
        P.op("vector", lambda v, _c0=col0: v.tensor_tensor(
            out=tc[:], in0=ta[:], in1=cos_t[:, _c0:_c0 + 512], op=ALU.mult),
            reads=[bta, bcos], writes=[btc])
        P.op("vector", lambda v, _c0=col0: v.tensor_tensor(
            out=td[:], in0=tb[:], in1=sin_t[:, _c0:_c0 + 512], op=ALU.mult),
            reads=[btb, bsin], writes=[btd])
        P.op("vector", lambda v: v.tensor_tensor(out=ta[:], in0=tc[:], in1=td[:], op=ALU.add),
             reads=[btc, btd], writes=[bta])
        P.op("scalar", lambda s: s.activation(out=sq[:], in_=ta[:], func=ACTF.Square),
             reads=[bta], writes=[bsq])
        P.op("tensor", lambda t: t.matmul(pn[:], ones_c[:], sq[:], start=True, stop=True),
             reads=[bsq, b_ones_c], writes=[bpn])
        P.op("vector", lambda v: v.tensor_copy(nrow[:], pn[:]), reads=[bpn], writes=[b_nrow])
        P.op("vector", lambda v, _n=norm_t: v.tensor_max(_n[:], _n[:], nrow[:]),
             reads=[b_nrow], writes=[bnorm2])
        split32(ta[:], bta, chh[:], chl[:], bchh, bchl, via_act=True)
        P.dma("scalar", dst_h[o, :, dst_col0:dst_col0 + 512], chh[:], reads=[bchh], writes=[bdh])
        P.dma("scalar", dst_l[o, :, dst_col0:dst_col0 + 512], chl[:], reads=[bchl], writes=[bdl])

    d = alloc_ffn_stack("a")
    stk1 = d["_stack"]

    for tt in range(4):
        t0 = tt * 512
        own_tt = tt >= 2
        j0 = (tt - 2) * 512

        x0, bx0 = d["x0tm"]
        ckt, b_ckt = d["ck"]
        skt, b_skt = d["sk"]
        P.dma("sync", ckt[:], cosk_d[:, t0:t0 + 512], writes=[b_ckt])
        P.dma("sync", skt[:], sink_d[:, t0:t0 + 512], writes=[b_skt])
        if own_tt:
            cqt, b_cqt = d["cq"]
            sqt, b_sqt = d["squ"]
            P.dma("sync", cqt[:], cosq_d[:, j0:j0 + 512], writes=[b_cqt])
            P.dma("sync", sqt[:], sinq_d[:, j0:j0 + 512], writes=[b_sqt])
        statc, bstat = d["statc"]
        sqtm, bsqtm = d["xn_h"]
        tc, btc = d["tc"]
        for stt in range(4):
            P.dma("sync", x0[:], xin[t0 + stt * 128: t0 + (stt + 1) * 128, :], writes=[bx0])
            P.op("vector", lambda v, _s=stt: v.reduce_sum(statc[:, _s:_s + 1], x0[:], axis=AX.X),
                 reads=[bx0], writes=[bstat])
            P.op("scalar", lambda s, _s=stt: s.activation(
                out=sqtm[:, 0:D].bitcast(BF16)[:, 0:D], in_=x0[:], func=ACTF.Square,
                accum_out=statc[:, 4 + _s:5 + _s]),
                reads=[bx0], writes=[bsqtm, bstat])
            P.op("scalar", lambda s, _s=stt: s.mul(
                out=statc[:, 8 + _s:9 + _s], in_=statc[:, _s:_s + 1], mul=INV_D),
                reads=[bstat], writes=[bstat])
            P.op("vector", lambda v, _s=stt: v.tensor_tensor(
                out=r1a[:], in0=statc[:, 8 + _s:9 + _s], in1=statc[:, 8 + _s:9 + _s], op=ALU.mult),
                reads=[bstat], writes=[b_r1a])
            P.op("vector", lambda v, _s=stt: v.scalar_tensor_tensor(
                out=r1b[:], in0=statc[:, 4 + _s:5 + _s], scalar=INV_D, in1=r1a[:],
                op0=ALU.mult, op1=ALU.subtract), reads=[bstat, b_r1a], writes=[b_r1b])
            P.op("scalar", lambda s: s.activation(out=r1b[:], in_=r1b[:], func=ACTF.Sqrt,
                                                  bias=epsc[:, 0:1]),
                 reads=[b_r1b, b_eps], writes=[b_r1b])
            P.op("vector", lambda v, _s=stt: v.reciprocal(statc[:, 12 + _s:13 + _s], r1b[:]),
                 reads=[b_r1b], writes=[bstat])
            P.op("vector", lambda v, _s=stt: v.tensor_tensor(
                out=statc[:, 8 + _s:9 + _s], in0=statc[:, 8 + _s:9 + _s],
                in1=statc[:, 12 + _s:13 + _s], op=ALU.mult), reads=[bstat], writes=[bstat])
            xds = d["xn_h"][0][:, 4096:8192].bitcast(F32)
            bxds = d["xn_h"][1]
            for grp in range(8):
                tp, btp = d["ptp"][grp % 2]

                def btr(t, _g=grp, _tp=tp):
                    r = None
                    for k in range(4):
                        c = _g * 4 + k
                        r = t.transpose(_tp[:, k * 128:(k + 1) * 128],
                                        x0[:, c * 128:(c + 1) * 128], t_id[:])
                    return r
                P.op("tensor", btr, reads=[bx0, b_id], writes=[btp])
                P.op("vector", lambda v, _tp=tp, _g=grp: v.tensor_copy(
                    xds[:, (_g % 4) * 512:((_g % 4) + 1) * 512], _tp[:]),
                    reads=[btp], writes=[bxds])
                if grp % 4 == 3:
                    half = grp // 4
                    P.dma("gpsimd",
                          x0sc[half * 16:(half + 1) * 16, :, t0 + stt * 128: t0 + (stt + 1) * 128]
                          .rearrange("c p t -> p c t"),
                          xds[:].rearrange("p (c t) -> p c t", c=16),
                          reads=[bxds], writes=[b_x0sc])
        for stt in range(4):
            for bank, col, (dst, bdst) in ((0, 12 + stt, (rrow, b_rrow)),
                                           (1, 8 + stt, (mrow, b_mrow))):
                tp, btp = d["ptp"][bank]
                P.op("tensor", lambda t, _c=col, _tp=tp: t.transpose(
                    _tp[0:1, 0:128], statc[:, _c:_c + 1], t_id[:]),
                    reads=[bstat, b_id], writes=[btp])
                P.op("vector", lambda v, _tp=tp, _s=stt, _d=dst: v.tensor_copy(
                    _d[:, _s * 128:(_s + 1) * 128], _tp[0:1, 0:128]),
                    reads=[btp], writes=[bdst])
        bcast_rows(d)
        norm_split(d, x0sc, b_x0sc, t0)
        if debug and tt == 0:
            P.dma("gpsimd", dbg_statc[:, :], statc[:], reads=[bstat])
            P.dma("gpsimd", dbg_rrow[:, :], rrow[:], reads=[b_rrow])
            P.dma("gpsimd", dbg_mrow[:, :], mrow[:], reads=[b_mrow])
            P.dma("gpsimd", dbg_rbc[:, :], d["rbc"][0][:], reads=[d["rbc"][1]])
            P.dma("gpsimd", dbg_mrbc[:, :], d["mrbc"][0][:], reads=[d["mrbc"][1]])
            P.dma("gpsimd", dbg_xnh[:, :], d["xn_h"][0][:], reads=[d["xn_h"][1]])
            P.dma("gpsimd", dbg_xnl[:, :], d["xn_l"][0][:], reads=[d["xn_l"][1]])
        router_and_h(d)
        if debug and tt == 0:
            P.dma("gpsimd", dbg_hp[:, :], d["hp_h"][0][:], reads=[d["hp_h"][1]])
            P.dma("gpsimd", dbg_wbc[:, :], d["wbc"][0][:], reads=[d["wbc"][1]])

        def spill_x1(o, td_t, btd, _t0=t0):
            P.dma("scalar", x1sc[o, :, _t0:_t0 + 512], td_t[:], reads=[btd], writes=[b_x1sc])
        ffn_base_loop(d, wb_d, x0sc, b_x0sc, t0, spill_x1)
        ln_rows(d)
        bcast_rows(d)
        norm_split(d, x1sc, b_x1sc, t0)

        def k_epi(g, pmt, bpm, _t0=t0, _ck=ckt, _bck=b_ckt, _sk=skt, _bsk=b_skt):
            rope_split_spill(d, g, pmt, bpm, _ck, _bck, _sk, _bsk, 0,
                             khs, b_khs, kls, b_kls, _t0, nk2, b_nk2)
        pipelined_proj(d, wk_d, NKV, 512, k_epi)

        def v_epi(c, pmt, bpm, _tt=tt):
            ta, bta = d["ta"]
            tb, btb = d["tb"]
            chh, bchh = d["ch_h"][c % 2]
            chl, bchl = d["ch_l"][c % 2]
            tp0, btp0 = d["ptp"][0]
            P.op("vector", lambda v, _p=pmt: v.tensor_copy(ta[:], _p[:]), reads=[bpm], writes=[bta])

            def btr(t):
                r = None
                for sub in range(4):
                    r = t.transpose(tp0[:, sub * 128:(sub + 1) * 128],
                                    ta[:, sub * 128:(sub + 1) * 128], t_id[:])
                return r
            P.op("tensor", btr, reads=[bta, b_id], writes=[btp0])
            P.op("vector", lambda v: v.tensor_copy(tb[:], tp0[:]), reads=[btp0], writes=[btb])
            split32(tb[:], btb, chh[:], chl[:], bchh, bchl, via_act=True)
            P.dma("scalar",
                  vth[_tt * 4:(_tt + 1) * 4, :, c * 128:(c + 1) * 128].rearrange("T p d -> p T d"),
                  chh[:].rearrange("p (T d) -> p T d", T=4), reads=[bchh], writes=[b_vth])
            P.dma("scalar",
                  vtl[_tt * 4:(_tt + 1) * 4, :, c * 128:(c + 1) * 128].rearrange("T p d -> p T d"),
                  chl[:].rearrange("p (T d) -> p T d", T=4), reads=[bchl], writes=[b_vtl])
        pipelined_proj(d, wv_d, NKV, 512, v_epi)

        if own_tt:
            def q_epi(g, pmt, bpm, _j0=j0, _cq=cqt, _bcq=b_cqt, _sq=sqt, _bsq=b_sqt):
                rope_split_spill(d, g, pmt, bpm, _cq, _bcq, _sq, _bsq, 0,
                                 qhs, b_qhs, qls, b_qls, _j0, nq2, b_nq2)
            pipelined_proj(d, wq_d, NCK, 512, q_epi)

    P.flush()
    stk1.close()
    P.barrier()

    ph2 = ExitStack()
    mask_t, b_mask = SB(ph2, "mask_t", [128, NB * NST * QB], F32)
    kb_h = [SB(ph2, f"kb_h{i}", [128, NST * 128], BF16) for i in range(2)]
    kb_l = [SB(ph2, f"kb_l{i}", [128, NST * 128], BF16) for i in range(2)]
    vt_h = [SB(ph2, f"vt_h{i}", [128, NST * 128], BF16) for i in range(2)]
    vt_l = [SB(ph2, f"vt_l{i}", [128, NST * 128], BF16) for i in range(2)]
    qg_h = [SB(ph2, f"qg_h{i}", [128, 4 * QB], BF16) for i in range(2)]
    qg_l = [SB(ph2, f"qg_l{i}", [128, 4 * QB], BF16) for i in range(2)]
    aht, _ = SB(ph2, "a_h", [128, NST * QB], BF16)
    alt, _ = SB(ph2, "a_l", [128, NST * QB], BF16)
    b_ah = [Buf() for _ in range(NST)]
    b_al = [Buf() for _ in range(NST)]
    em = [SB(ph2, f"em{i}", [128, QB], F32) for i in range(2)]
    em2 = [SB(ph2, f"em2{i}", [128, QB], F32) for i in range(2)]
    rb_t, b_rb = SB(ph2, "rb_sb", [128, QB], F32)
    of_t, b_of = SB(ph2, "of32", [128, QB], F32)
    ohh, b_ohh = SB(ph2, "oh_sb", [128, QB], BF16)
    oll, b_oll = SB(ph2, "ol_sb", [128, QB], BF16)
    rr2, b_rr2 = SB(ph2, "rrow2", [1, QB], F32)
    pst = [PS(ph2, f"pst{i}", [128, QB]) for i in range(4)]
    pavt, b_pav = PS(ph2, "pav", [128, QB])
    psrt, b_psr = PS(ph2, "psr", [1, QB])
    pbct, b_pbc = PS(ph2, "pbc", [128, QB])

    P.op("vector", lambda v: v.reduce_max(r1a[0:1, 0:1], nq2[:], axis=AX.X),
         reads=[b_nq2], writes=[b_r1a])
    P.op("vector", lambda v: v.reduce_max(r1b[0:1, 0:1], nk2[:], axis=AX.X),
         reads=[b_nk2], writes=[b_r1b])
    P.op("vector", lambda v: v.tensor_tensor(out=r1a[0:1, 0:1], in0=r1a[0:1, 0:1],
                                             in1=r1b[0:1, 0:1], op=ALU.mult),
         reads=[b_r1b, b_r1a], writes=[b_r1a])
    P.op("vector", lambda v: v.tensor_scalar_mul(r1a[0:1, 0:1], r1a[0:1, 0:1], 1.0404),
         reads=[b_r1a], writes=[b_r1a])
    P.op("scalar", lambda s: s.activation(out=r1a[0:1, 0:1], in_=r1a[0:1, 0:1], func=ACTF.Sqrt),
         reads=[b_r1a], writes=[b_r1a])
    P.op("scalar", lambda s: s.mul(out=r1a[0:1, 0:1], in_=r1a[0:1, 0:1], mul=-1.0),
         reads=[b_r1a], writes=[b_r1a])
    P.op("tensor", lambda t: t.matmul(pbct[:, 0:1], ones_r[:], r1a[0:1, 0:1], start=True, stop=True),
         reads=[b_r1a, b_ones_r], writes=[b_pbc])
    P.op("vector", lambda v: v.tensor_copy(biasc[:], pbct[:, 0:1]), reads=[b_pbc], writes=[b_biasc])
    P.dma("sync", mask_t[:], masks_d[:, :], writes=[b_mask])

    for B in range(NB):
        s0 = 2 * B * 128
        for g in range(NKV):
            slot = g % 2
            kh, bkh = kb_h[slot]
            kl, bkl = kb_l[slot]
            vh, bvh = vt_h[slot]
            vl, bvl = vt_l[slot]
            qh_t, bqh = qg_h[slot]
            ql_t, bql = qg_l[slot]
            P.dma("sync", kh[:], khs[g, :, s0:s0 + NST * 128], reads=[b_khs], writes=[bkh])
            P.dma("sync", kl[:], kls[g, :, s0:s0 + NST * 128], reads=[b_kls], writes=[bkl])
            P.fill("sync", [(vh[:, st * 128:(st + 1) * 128],
                             vth[2 * B + st, :, g * 128:(g + 1) * 128]) for st in range(NST)],
                   reads=[b_vth], writes=[bvh])
            P.fill("sync", [(vl[:, st * 128:(st + 1) * 128],
                             vtl[2 * B + st, :, g * 128:(g + 1) * 128]) for st in range(NST)],
                   reads=[b_vtl], writes=[bvl])
            P.fill("sync", [(qh_t[:, j * QB:(j + 1) * QB],
                             qhs[g * 4 + j, :, B * QB:(B + 1) * QB]) for j in range(4)],
                   reads=[b_qhs], writes=[bqh])
            P.fill("sync", [(ql_t[:, j * QB:(j + 1) * QB],
                             qls[g * 4 + j, :, B * QB:(B + 1) * QB]) for j in range(4)],
                   reads=[b_qls], writes=[bql])
            for j in range(4):
                qh_ap = qh_t[:, j * QB:(j + 1) * QB]
                ql_ap = ql_t[:, j * QB:(j + 1) * QB]
                hq = g * 4 + j
                for step in range(NST + 2):
                    if step < NST:
                        st = step
                        ps_t, bps = pst[st % 4]

                        def bsc(t, _st=st, _ps=ps_t, _qh=qh_ap, _ql=ql_ap, _kh=kh, _kl=kl):
                            t.matmul(_ps[:], _kh[:, _st * 128:(_st + 1) * 128], _qh,
                                     start=True, stop=False)
                            t.matmul(_ps[:], _kl[:, _st * 128:(_st + 1) * 128], _qh,
                                     start=False, stop=False)
                            return t.matmul(_ps[:], _kh[:, _st * 128:(_st + 1) * 128], _ql,
                                            start=False, stop=True)
                        P.op("tensor", bsc, reads=[bkh, bkl, bqh, bql], writes=[bps])
                        emt, bem = em[st % 2]
                        em2t, bem2 = em2[st % 2]
                        P.op("scalar", lambda s, _p=ps_t, _e=emt: s.activation(
                            out=_e[:], in_=_p[:], func=ACTF.Exp, bias=biasc[:, 0:1]),
                            reads=[bps, b_biasc], writes=[bem])
                        mslice = mask_t[:, (B * NST + st) * QB:(B * NST + st + 1) * QB]
                        P.op("vector", lambda v, _e=emt, _m=mslice, _st=st: v.tensor_tensor(
                            out=aht[:, _st * QB:(_st + 1) * QB], in0=_e[:], in1=_m, op=ALU.mult),
                            reads=[bem, b_mask], writes=[b_ah[st]])
                        P.op("vector", lambda v, _e=emt, _st=st, _e2=em2t: v.tensor_tensor(
                            out=_e2[:], in0=_e[:], in1=aht[:, _st * QB:(_st + 1) * QB],
                            op=ALU.subtract),
                            reads=[bem, b_ah[st]], writes=[bem2])
                        P.op("vector", lambda v, _m=mslice, _st=st, _e2=em2t: v.tensor_tensor(
                            out=alt[:, _st * QB:(_st + 1) * QB], in0=_e2[:], in1=_m, op=ALU.mult),
                            reads=[bem2, b_mask], writes=[b_al[st]])
                    if step >= 2:
                        st = step - 2
                        def bsum(t, _st=st):
                            t.matmul(psrt[:], ones_b[:], aht[:, _st * QB:(_st + 1) * QB],
                                     start=(_st == 0), stop=False, skip_group_check=True)
                            return t.matmul(psrt[:], ones_b[:], alt[:, _st * QB:(_st + 1) * QB],
                                            start=False, stop=(_st == NST - 1),
                                            skip_group_check=True)
                        P.op("tensor", bsum,
                             reads=[b_ah[st], b_al[st], b_ones_b],
                             writes=([b_psr] if st == NST - 1 else []),
                             extra=(([b_psr.w] if b_psr.w else []) + list(b_psr.r) if st == 0 else []))

                def bav(t, _vh=vh, _vl=vl):
                    r = None
                    for st in range(NST):
                        ah_ap = aht[:, st * QB:(st + 1) * QB]
                        al_ap = alt[:, st * QB:(st + 1) * QB]
                        t.matmul(pavt[:], _vh[:, st * 128:(st + 1) * 128], ah_ap,
                                 start=(st == 0), stop=False, skip_group_check=True)
                        t.matmul(pavt[:], _vl[:, st * 128:(st + 1) * 128], ah_ap,
                                 start=False, stop=False, skip_group_check=True)
                        r = t.matmul(pavt[:], _vh[:, st * 128:(st + 1) * 128], al_ap,
                                     start=False, stop=(st == NST - 1), skip_group_check=True)
                    return r
                P.op("tensor", bav, reads=[bvh, bvl] + b_ah + b_al, writes=[b_pav])
                P.op("vector", lambda v: v.tensor_copy(rr2[:], psrt[:]),
                     reads=[b_psr], writes=[b_rr2])
                P.op("vector", lambda v: v.reciprocal(rr2[:], rr2[:]), reads=[b_rr2], writes=[b_rr2])
                P.op("tensor", lambda t: t.matmul(pbct[:], ones_r[:], rr2[:], start=True, stop=True),
                     reads=[b_rr2, b_ones_r], writes=[b_pbc])
                P.op("vector", lambda v: v.tensor_copy(rb_t[:], pbct[:]), reads=[b_pbc], writes=[b_rb])
                P.op("vector", lambda v: v.tensor_tensor(out=of_t[:], in0=pavt[:], in1=rb_t[:],
                                                         op=ALU.mult),
                     reads=[b_pav, b_rb], writes=[b_of])
                P.op("scalar", lambda s: s.copy(out=ohh[:], in_=of_t[:]), reads=[b_of], writes=[b_ohh])
                P.op("vector", lambda v: v.tensor_tensor(out=oll[:], in0=of_t[:], in1=ohh[:],
                                                         op=ALU.subtract),
                     reads=[b_of, b_ohh], writes=[b_oll])
                P.dma("gpsimd", osh[hq, :, B * QB:(B + 1) * QB], ohh[:], reads=[b_ohh], writes=[b_osh])
                P.dma("gpsimd", osl[hq, :, B * QB:(B + 1) * QB], oll[:], reads=[b_oll], writes=[b_osl])

    P.flush()
    ph2.close()
    P.barrier()

    d = alloc_ffn_stack("b")
    stk3 = d["_stack"]
    for tau in range(2):
        j0 = tau * 512
        col1 = 1024 + j0
        xnh, bxnh = d["xn_h"]
        xnl, bxnl = d["xn_l"]
        P.fill("sync", [(xnh[:, c * 512:(c + 1) * 512], osh[c, :, j0:j0 + 512])
                        for c in range(NCK)], reads=[b_osh], writes=[bxnh])
        P.fill("sync", [(xnl[:, c * 512:(c + 1) * 512], osl[c, :, j0:j0 + 512])
                        for c in range(NCK)], reads=[b_osl], writes=[bxnl])
        tc, btc = d["tc"]

        def o_epi(o, pmt, bpm, _j0=j0, _col1=col1):
            xr, bxr = d["xr"][o % 2]
            P.dma("sync", xr[:], x1sc[o, :, _col1:_col1 + 512], reads=[b_x1sc], writes=[bxr])
            P.op("vector", lambda v, _p=pmt, _x=xr: v.tensor_tensor(
                out=tc[:], in0=_p[:], in1=_x[:], op=ALU.add), reads=[bpm, bxr], writes=[btc])
            stat_mms(d, tc[:], btc, o == 0, o == NCK - 1)
            P.dma("scalar", x2sc[o, :, _j0:_j0 + 512], tc[:], reads=[btc], writes=[b_x2sc])
        pipelined_proj(d, wo_d, NCK, 512, o_epi)
        ln_rows(d)
        bcast_rows(d)
        norm_split(d, x2sc, b_x2sc, j0)
        router_and_h(d)

        def out_cb(o, td_t, btd, _j0=j0):
            ta, bta = d["ta"]
            for sub in range(4):
                tp, btp = d["ptp"][sub % 2]
                P.op("tensor", lambda t, _tp=tp, _s=sub, _c=td_t: t.transpose(
                    _tp[:, 0:128], _c[:, _s * 128:(_s + 1) * 128], t_id[:]),
                    reads=[btd, b_id], writes=[btp])
                P.op("vector", lambda v, _tp=tp, _s=sub: v.tensor_copy(
                    ta[:, _s * 128:(_s + 1) * 128], _tp[:, 0:128]),
                    reads=[btp], writes=[bta])
            for sub in range(4):
                P.dma("sync", yout[_j0 + sub * 128:_j0 + (sub + 1) * 128, o * 128:(o + 1) * 128],
                      ta[:, sub * 128:(sub + 1) * 128], reads=[bta])
        ffn_base_loop(d, wb_d, x2sc, b_x2sc, j0, out_cb, stats=False)

    P.flush()
    stk3.close()

    P._emit("gpsimd", lambda e: e.nop(),
            [("sy", P.cnt["sy"]), ("gq", P.cnt["gq"]), ("te", P.cnt["te"]),
             ("ve", P.cnt["ve"]), ("se", P.cnt["se"])], "gq", 16)
    P.flush()
    return nc


_NC_CACHE = None


def _host_tables(h):
    inv = 10000.0 ** (-np.arange(64, dtype=np.float64) / 64.0)
    sm = 1.0 / np.sqrt(128.0)

    def tables(pos, scale):
        ang = pos[None, :].astype(np.float64) * inv[:, None]
        cos = np.cos(ang) * scale
        sin = np.sin(ang) * scale
        cos_t = np.concatenate([cos, cos], axis=0).astype(np.float32)
        sin_t = np.concatenate([-sin, sin], axis=0).astype(np.float32)
        return np.ascontiguousarray(cos_t), np.ascontiguousarray(sin_t)

    pos_own = np.arange(OWN, dtype=np.float64) + h * OWN
    pos_ctx = (np.arange(CTX, dtype=np.float64) if h == 1
               else np.concatenate([np.arange(OWN), np.arange(OWN)]).astype(np.float64))
    cosq, sinq = tables(pos_own, sm)
    cosk, sink = tables(pos_ctx, 1.0)
    return cosq, sinq, cosk, sink


def _host_masks(h):
    m = np.zeros((NB * NST, 128, QB), dtype=np.float32)
    lo = 1024 if h == 0 else 0
    for B in range(NB):
        for st in range(NST):
            s_idx = (2 * B + st) * 128 + np.arange(128)[:, None]
            j_idx = B * QB + np.arange(QB)[None, :]
            ok = (s_idx > j_idx) & (s_idx <= j_idx + 1024) & (s_idx >= lo)
            m[B * NST + st] = ok.astype(np.float32)
    return np.ascontiguousarray(m.transpose(1, 0, 2).reshape(128, NB * NST * QB))


def kernel(L0_attn_x, L0_attn_wq, L0_attn_wk, L0_attn_wv, L0_attn_wo,
           L0_ffn_W_base, L0_ffn_b_base, L0_ffn_W_router, L0_ffn_lora_A, L0_ffn_lora_B):
    global _NC_CACHE
    x = np.ascontiguousarray(np.asarray(L0_attn_x, dtype=np.float32))
    wq = np.ascontiguousarray(np.asarray(L0_attn_wq, dtype=np.float32))
    wk = np.ascontiguousarray(np.asarray(L0_attn_wk, dtype=np.float32))
    wv = np.ascontiguousarray(np.asarray(L0_attn_wv, dtype=np.float32))
    wo = np.ascontiguousarray(np.asarray(L0_attn_wo, dtype=np.float32))
    wb = np.ascontiguousarray(np.asarray(L0_ffn_W_base, dtype=np.float32).T)
    bbv = np.asarray(L0_ffn_b_base, dtype=np.float32)
    bb = np.ascontiguousarray(bbv.reshape(NCK, 128).T)
    wrv = np.asarray(L0_ffn_W_router, dtype=np.float32).T
    wr = np.ascontiguousarray(wrv.reshape(NCK, 128, 8).transpose(1, 0, 2).reshape(128, NCK * 8))
    lav = np.asarray(L0_ffn_lora_A, dtype=np.float32).reshape(128, D).T
    la = np.ascontiguousarray(lav.reshape(NCK, 128, 128).transpose(1, 0, 2).reshape(128, NCK * 128))
    lb = np.ascontiguousarray(np.asarray(L0_ffn_lora_B, dtype=np.float32).transpose(0, 2, 1).reshape(128, D))
    idn = np.eye(128, dtype=np.float32)
    e8 = np.zeros((8, 128), dtype=np.float32)
    for e in range(8):
        e8[e, e * 16:(e + 1) * 16] = 2.0

    if _NC_CACHE is None:
        _NC_CACHE = build_program()
    nc = _NC_CACHE

    in_maps = _in_maps(x, wq, wk, wv, wo, wb, wr, la, lb, bb, idn, e8)
    res = run_bass_kernel_spmd(nc, in_maps, core_ids=list(range(8)))
    out = np.empty((4, CTX, D), dtype=np.float32)
    for i in range(8):
        b, h = i // 2, i % 2
        out[b, h * OWN:(h + 1) * OWN, :] = res.results[i]["yout"]
    return out


def _in_maps(x, wq, wk, wv, wo, wb, wr, la, lb, bb, idn, e8):
    in_maps = []
    for i in range(8):
        b, h = i // 2, i % 2
        xc = x[b] if h == 1 else np.concatenate([x[b, :OWN], x[b, :OWN]], axis=0)
        cosq, sinq, cosk, sink = _host_tables(h)
        in_maps.append({
            "xin": np.ascontiguousarray(xc), "wq": wq, "wk": wk, "wv": wv, "wo": wo,
            "wb": wb, "wr": wr, "la": la, "lb": lb, "bb": bb,
            "cosq": cosq, "sinq": sinq, "cosk": cosk, "sink": sink,
            "masks": _host_masks(h), "idn": idn, "e8": e8,
        })
    return in_maps
